# version 18
# baseline (speedup 1.0000x reference)
import numpy as np
import ml_dtypes
import concourse.bass as bass
import concourse.mybir as mybir
from concourse.bass_utils import run_bass_kernel_spmd
from contextlib import ExitStack

F32 = mybir.dt.float32
BF16 = mybir.dt.bfloat16
AF = mybir.ActivationFunctionType
ALU = mybir.AluOpType
AX = mybir.AxisListType

D = 4096
PB = 128
CH = D // PB
NCORE = 8
NT = 8
QKVW = 6144
NEXP = 8
L = 4096
EPS = 1e-5
ATT_SCALE = float(1.0 / np.sqrt(128.0))
CPS = 4
COLW = 2048
NWBUF = 3

SUBLAYERS = (True, True, True, True)

ENGINES = ("sync", "gpsimd", "scalar", "vector", "tensor")


class Sem:
    def __init__(self, handle, name):
        self.h = handle
        self.name = name
        self.n = 0


class Prog:
    def __init__(self):
        self.ops = {e: [] for e in ENGINES}
        self.waited = {e: {} for e in ENGINES}

    def emit(self, eng, fn):
        self.ops[eng].append(fn)

    def wait(self, eng, sem, val):
        if val > 0 and self.waited[eng].get(sem.name, -1) < val:
            self.waited[eng][sem.name] = val
            self.ops[eng].append(lambda e, s=sem.h, v=val: e.wait_ge(s, v))

    def replay(self, block):
        for eng in ENGINES:
            ops = self.ops[eng]
            getattr(block, eng)(lambda e, ops=ops: [op(e) for op in ops])


def _host_consts():
    j = np.arange(L)
    t32 = (j / L).astype(np.float32)
    Om = np.exp(np.complex64(-2j * np.pi) * t32).astype(np.complex64)
    om = ((1 - Om) / (1 + Om)).astype(np.complex64)
    cc = (np.complex64(2.0) / (1 + Om)).astype(np.complex64)

    i64 = np.arange(64)
    E1 = np.exp(2j * np.pi * np.outer(i64, i64) / 64.0)
    Wt = np.exp(2j * np.pi * np.outer(i64, i64) / 4096.0)
    E2s = E1 / 4096.0

    f32 = lambda a: np.ascontiguousarray(a, dtype=np.float32)
    return {
        "om_re": f32(np.broadcast_to(om.real, (64, L))),
        "om_im": f32(np.broadcast_to(om.imag, (64, L))),
        "c_re": f32(cc.real.reshape(64, 64)),
        "c_im": f32(cc.imag.reshape(64, 64)),
        "E1_re": f32(E1.real), "E1_im": f32(E1.imag), "E1_imn": f32(-E1.imag),
        "W_re": f32(Wt.real), "W_im": f32(Wt.imag),
        "E2_re": f32(E2s.real), "E2_imn": f32(-E2s.imag),
        "ident_bf": np.eye(128, dtype=ml_dtypes.bfloat16),
        "prev_bf": np.eye(128, dtype=ml_dtypes.bfloat16)[::-1].copy(),
        "ones_f": np.ones((1, 128), dtype=np.float32),
    }


def build_program():
    nc = bass.Bass("TRN2", target_bir_lowering=False)

    x_in = nc.dram_tensor("x_in", [2, 1024, D], F32, kind="ExternalInput")
    x_out = nc.dram_tensor("x_out", [2, 1024, D], F32, kind="ExternalOutput")
    wqkvT = nc.dram_tensor("wqkvT", [D, QKVW], F32, kind="ExternalInput")
    wprojT = nc.dram_tensor("wprojT", [D, D], F32, kind="ExternalInput")
    wffnT = nc.dram_tensor("wffnT", [D, D], F32, kind="ExternalInput")
    gateT = nc.dram_tensor("gateT", [D, NEXP], F32, kind="ExternalInput")
    ics = nc.dram_tensor("ics", [NEXP, D], F32, kind="ExternalInput")
    ocs = nc.dram_tensor("ocs", [NEXP, D], F32, kind="ExternalInput")
    bias_in = nc.dram_tensor("bias_in", [1, D], F32, kind="ExternalInput")
    s4_small = {}
    for nm in ["lam_re", "lam_im", "p_re", "p_im", "q_re", "q_im",
               "ct_re", "ct_im", "bp_in"]:
        s4_small[nm] = nc.dram_tensor(nm, [64, 1], F32, kind="ExternalInput")
    vcT_re = nc.dram_tensor("vcT_re", [64, 64], F32, kind="ExternalInput")
    vcT_im = nc.dram_tensor("vcT_im", [64, 64], F32, kind="ExternalInput")
    d_in = nc.dram_tensor("d_in", [1, 1], F32, kind="ExternalInput")
    ls_in = nc.dram_tensor("ls_in", [1, 1], F32, kind="ExternalInput")
    cn = {}
    for name, shp, dt in [
        ("om_re", [64, L], F32), ("om_im", [64, L], F32),
        ("c_re", [64, 64], F32), ("c_im", [64, 64], F32),
        ("E1_re", [64, 64], F32), ("E1_im", [64, 64], F32), ("E1_imn", [64, 64], F32),
        ("W_re", [64, 64], F32), ("W_im", [64, 64], F32),
        ("E2_re", [64, 64], F32), ("E2_imn", [64, 64], F32),
        ("ident_bf", [128, 128], BF16), ("prev_bf", [128, 128], BF16),
        ("ones_f", [1, 128], F32),
    ]:
        cn[name] = nc.dram_tensor(name, shp, dt, kind="ExternalInput")
    qkv_bf = nc.dram_tensor("qkv_bf", [D, QKVW], BF16, kind="Internal")
    proj_bf = nc.dram_tensor("proj_bf", [D, D], BF16, kind="Internal")
    ffn_bf = nc.dram_tensor("ffn_bf", [D, D], BF16, kind="Internal")
    kk_scr = nc.dram_tensor("kk_scr", [8, L], F32, kind="Internal")
    kpad = nc.dram_tensor("kpad", [127 + L], BF16, kind="Internal")

    P = Prog()

    with ExitStack() as stack:
        def sb(name, shape, dt):
            return stack.enter_context(nc.sbuf_tensor(name, shape, dt))

        def mksem(name):
            return Sem(stack.enter_context(nc.semaphore(name)), name)

        res = [sb(f"res{b}", [PB, D], F32) for b in range(2)]
        actT = [sb(f"actT{b}", [PB, D], BF16) for b in range(2)]
        wbuf = [sb(f"wbuf{i}", [PB, CPS * COLW], BF16) for i in range(NWBUF)]
        bias_b = sb("bias_b", [PB, D], F32)
        ttoep = sb("ttoep", [PB, D], BF16)
        qkv_sb = [sb(f"qkvsb{b}", [PB, QKVW], BF16) for b in range(2)]
        osc_bf = [sb(f"oscbf{b}", [PB, D], BF16) for b in range(2)]
        ln_bf = sb("ln_bf", [PB, D], BF16)
        tr2 = sb("tr2", [PB, D // 2], F32)
        prodb = sb("prodb", [PB, 1024], F32)
        gate_sb = sb("gate_sb", [PB, CH * NEXP], BF16)
        ics_sb = sb("ics_sb", [NEXP, D], BF16)
        ocs_sb = sb("ocs_sb", [NEXP, D], BF16)
        ident_sb = sb("ident_sb", [128, 128], BF16)
        prev_sb = sb("prev_sb", [128, 128], BF16)
        ones_sb = sb("ones_sb", [1, 128], F32)
        routT = [sb(f"routT{b}", [NEXP, PB], BF16) for b in range(2)]
        rout_f = sb("rout_f", [PB, NEXP], F32)
        rout_bf = sb("rout_bf", [PB, NEXP], BF16)
        stat = sb("stat", [PB, 24], F32)
        epscol = sb("epscol", [PB, 1], F32)
        dcol = sb("dcol", [PB, 1], F32)
        alph = sb("alph", [PB, 64], F32)
        kg_a2 = sb("kg_a2", [64, 1], F32)
        kg_dre = tr2[0:64, 1024:2048]
        kg_dim = prodb[0:64, 0:1024]
        kg_t = qkv_sb[1][:].bitcast(F32)[0:64, 2048:3072]
        kk_small = tr2[0:4, 0:1024]
        kgv = sb("kgv", [64, 24], F32)
        HL = L // 4
        kg64 = sb("kg64", [64, 64 * 26], F32)
        kmat_bf = sb("kmat_bf", [64, 64], BF16)
        zrow = sb("zrow", [1, 128], BF16)
        s4c_sb = sb("s4c_sb", [64, 12], F32)
        misc1 = sb("misc1", [1, 8], F32)
        hank = qkv_sb[1]
        dkbuf = tr2[:, 0:1024]
        dvbuf = tr2[:, 1024:2048]

        PS = [stack.enter_context(nc.psum_tensor(f"PS{bb}", [PB, 2048], F32))
              for bb in range(2)]

        def psv(bb, g, off=0, w=512):
            return PS[bb][:, g * 512 + off:g * 512 + off + w]

        def ps_bf(bb, g, cols=128, off=0):
            return PS[bb][:].bitcast(BF16)[:, g * 1024 + off:g * 1024 + off + cols]

        s_x = mksem("s_x")
        s_w = [mksem(f"s_w{i}") for i in range(NWBUF)]
        s_wfree = mksem("s_wfree")
        s_pe = mksem("s_pe")
        s_dve = mksem("s_dve")
        s_act = mksem("s_act")
        s_pool = mksem("s_pool")
        s_ppq = [mksem(f"s_ppq{i}") for i in range(3)]
        s_ppp = [mksem(f"s_ppp{i}") for i in range(2)]
        s_ldp = mksem("s_ldp")
        s_st = [mksem("s_st0"), mksem("s_st1")]
        s_ld = mksem("s_ld")
        s_kb = mksem("s_kb")
        s_kp = mksem("s_kp")
        s_atn = mksem("s_atn")

        def pe(fn, inc=False):
            if inc:
                s_pe.n += 1
                P.emit("tensor", lambda e, fn=fn: fn(e).then_inc(s_pe.h, 1))
            else:
                P.emit("tensor", fn)
            return s_pe.n

        def dve(fn, inc=True):
            P.wait("vector", s_dve, s_dve.n)
            s_dve.n += 1
            P.emit("vector", lambda e, fn=fn: fn(e).then_inc(s_dve.h, 1))
            return s_dve.n

        def act(fn, inc=True):
            P.wait("scalar", s_act, s_act.n)
            s_act.n += 1
            P.emit("scalar", lambda e, fn=fn: fn(e).then_inc(s_act.h, 1))
            return s_act.n

        def dma(eng, fn, sem):
            sem.n += 16
            P.emit(eng, lambda e, fn=fn: fn(e).then_inc(sem.h, 16))
            return sem.n

        lnbf_pe = [0]
        last_rtp = [0]
        osc_gate = [False]

        bank_free = {(bb, g): ("dve", 0) for bb in range(2) for g in range(4)}

        def set_bank(bb, g, eng, val):
            bank_free[(bb, g)] = (eng, val)

        def gate_bank(bb, g):
            eng, val = bank_free[(bb, g)]
            P.wait("tensor", s_dve if eng == "dve" else s_act, val)

        def gate_banks_for_pe():
            for bb in range(2):
                for g in range(4):
                    gate_bank(bb, g)

        dma("sync", lambda e: e.dma_start(out=ident_sb[:], in_=cn["ident_bf"].ap()), s_ld)
        dma("sync", lambda e: e.dma_start(out=prev_sb[:], in_=cn["prev_bf"].ap()), s_ld)
        dma("sync", lambda e: e.dma_start(out=ones_sb[:], in_=cn["ones_f"].ap()), s_ld)
        dma("sync", lambda e: e.dma_start(out=bias_b[0:1, :], in_=bias_in.ap()), s_ld)

        KG = {}
        _slot = [0]

        def kg_new(name):
            KG[name] = _slot[0]
            _slot[0] += 1
            return kg64[:, KG[name] * 64:(KG[name] + 1) * 64]

        def kg(name):
            return kg64[:, KG[name] * 64:(KG[name] + 1) * 64]

        for name in ["c_re", "c_im", "E1_re", "E1_im", "E1_imn", "W_re", "W_im",
                     "E2_re", "E2_imn"]:
            dst = kg_new(name)
            dma("sync", lambda e, d=dst, s=cn[name]: e.dma_start(out=d, in_=s.ap()), s_ld)
        dst = kg_new("vc_re")
        dma("sync", lambda e, d=dst: e.dma_start(out=d, in_=vcT_re.ap()), s_ld)
        dst = kg_new("vc_im")
        dma("sync", lambda e, d=dst: e.dma_start(out=d, in_=vcT_im.ap()), s_ld)
        for nm in ["at_re", "at_im", "Y_re", "Y_im", "Z_re", "Z_im",
                   "k00re", "k01re", "k10re", "k11re", "k00im", "k01im",
                   "k10im", "k11im"]:
            kg_new(nm)

        S4C = {}
        for i, nm in enumerate(["lam_re", "lam_im", "p_re", "p_im", "q_re",
                                "q_im", "ct_re", "ct_im", "bp_in"]):
            S4C[nm] = i
            dma("sync", lambda e, i=i, t=s4_small[nm]: e.dma_start(
                out=s4c_sb[:, i:i + 1], in_=t.ap()), s_ld)

        def s4c(nm):
            return s4c_sb[:, S4C[nm]:S4C[nm] + 1]

        dma("sync", lambda e: e.dma_start(out=misc1[0:1, 0:1], in_=d_in.ap()), s_ld)
        dma("sync", lambda e: e.dma_start(out=misc1[0:1, 1:2], in_=ls_in.ap()), s_ld)
        dma("gpsimd", lambda e: e.dma_start(
            out=gate_sb.ap().rearrange("p (c x) -> p c x", c=CH),
            in_=bass.AP(gateT, 0, [[NEXP, 128], [128 * NEXP, CH], [1, NEXP]])), s_ldp)
        dma("gpsimd", lambda e: e.dma_start(out=ics_sb[:], in_=ics.ap()), s_ldp)
        dma("gpsimd", lambda e: e.dma_start(out=ocs_sb[:], in_=ocs.ap()), s_ldp)
        LD_FINAL = s_ld.n
        LD_POOL = s_ldp.n
        for eng in ("vector", "scalar", "tensor"):
            P.wait(eng, s_ld, LD_FINAL)
            P.wait(eng, s_ldp, LD_POOL)

        dma("gpsimd", lambda e: e.dma_start(out=res[0][:], in_=x_in.ap()[0, 0:128, :]), s_x)
        dma("gpsimd", lambda e: e.dma_start(out=res[1][:], in_=x_in.ap()[1, 0:128, :]), s_x)

        dve(lambda e: e.memset(epscol[:], EPS))
        zrow_done = dve(lambda e: e.memset(zrow[:], 0.0), inc=True)

        pe_n = pe(lambda e: e.matmul(PS[0][:, 0:1], ones_sb[:], misc1[0:1, 0:1],
                                     start=True, stop=True), inc=True)
        P.wait("vector", s_pe, pe_n)
        dve(lambda e: e.tensor_copy(dcol[:], PS[0][:, 0:1]))

        act(lambda e: e.activation(misc1[0:1, 2:3], misc1[0:1, 1:2], AF.Exp, scale=-1.0))
        a2_done = act(lambda e: e.mul(misc1[0:1, 3:4], misc1[0:1, 2:3], 2.0), inc=True)
        P.wait("tensor", s_act, a2_done)
        pe_n = pe(lambda e: e.matmul(PS[0][0:64, 512:513], ones_sb[:, 0:64],
                                     misc1[0:1, 3:4], start=True, stop=True), inc=True)
        P.wait("vector", s_pe, pe_n)
        a2col_done = dve(lambda e: e.tensor_copy(kg_a2[:], PS[0][0:64, 512:513]), inc=True)

        pe(lambda e: e.matmul(PS[0][0:64, 1024:1025], kg("vc_re"), s4c("bp_in"),
                              start=True, stop=True))
        pe_n = pe(lambda e: e.matmul(PS[0][0:64, 1536:1537], kg("vc_im"), s4c("bp_in"),
                                     start=True, stop=True), inc=True)
        P.wait("vector", s_pe, pe_n)
        dve(lambda e: e.tensor_copy(kgv[:, 0:1], PS[0][0:64, 1024:1025]))
        dve(lambda e: e.tensor_copy(kgv[:, 1:2], PS[0][0:64, 1536:1537]))

        def cmul_conj(dr, di, are, aim, bre, bim):
            dve(lambda e: e.tensor_mul(kgv[:, 20:21], are, bre))
            dve(lambda e: e.tensor_mul(dr, aim, bim))
            dve(lambda e: e.tensor_add(dr, dr, kgv[:, 20:21]))
            dve(lambda e: e.tensor_mul(kgv[:, 20:21], are, bim))
            dve(lambda e: e.tensor_mul(di, aim, bre))
            dve(lambda e: e.tensor_sub(di, kgv[:, 20:21], di))

        cmul_conj(kgv[:, 2:3], kgv[:, 6:7], s4c("ct_re"), s4c("ct_im"), kgv[:, 0:1], kgv[:, 1:2])
        cmul_conj(kgv[:, 3:4], kgv[:, 7:8], s4c("ct_re"), s4c("ct_im"), s4c("p_re"), s4c("p_im"))
        cmul_conj(kgv[:, 4:5], kgv[:, 8:9], s4c("q_re"), s4c("q_im"), kgv[:, 0:1], kgv[:, 1:2])
        cmul_conj(kgv[:, 5:6], kgv[:, 9:10], s4c("q_re"), s4c("q_im"), s4c("p_re"), s4c("p_im"))
        vv_ok = dve(lambda e: e.tensor_scalar(kgv[:, 10:14], kgv[:, 6:10], -1.0, None, ALU.mult),
                    inc=True)
        t2v = tr2[0:64, 0:HL]
        for hf in range(4):
            if hf == 0:
                P.wait("sync", s_dve, vv_ok)
            else:
                P.wait("sync", s_dve, s_dve.n)
            ld0 = dma("sync", lambda e, hf=hf: e.dma_start(
                out=kg_dre[:], in_=cn["om_re"].ap()[:, hf * HL:(hf + 1) * HL]), s_kb)
            ld1 = dma("sync", lambda e, hf=hf: e.dma_start(
                out=kg_dim[:], in_=cn["om_im"].ap()[:, hf * HL:(hf + 1) * HL]), s_kb)
            P.wait("vector", s_kb, ld1)
            dve(lambda e: e.tensor_scalar(kg_dre[:], kg_dre[:], kg_a2[:], s4c("lam_re"),
                                          ALU.mult, ALU.subtract))
            dn_dve = dve(lambda e: e.tensor_scalar(kg_dim[:], kg_dim[:], kg_a2[:], s4c("lam_im"),
                                                   ALU.mult, ALU.subtract), inc=True)
            P.wait("scalar", s_dve, dn_dve)
            sq_done = act(lambda e: e.activation(kg_t[:], kg_dre[:], AF.Square), inc=True)
            P.wait("vector", s_act, sq_done)
            dve(lambda e: e.tensor_mul(t2v, kg_dim[:], kg_dim[:]))
            dve(lambda e: e.tensor_add(kg_t[:], kg_t[:], t2v))
            dve(lambda e: e.reciprocal(kg_t[:], kg_t[:]))
            dve(lambda e: e.tensor_mul(kg_dre[:], kg_dre[:], kg_t[:]))
            inv_done = dve(lambda e: e.scalar_tensor_tensor(kg_dim[:], kg_dim[:], -1.0, kg_t[:],
                                                            ALU.mult, ALU.mult), inc=True)
            P.wait("tensor", s_dve, inv_done)
            for jc in range(2):
                sl = slice(jc * 512, (jc + 1) * 512)
                gcol = hf * HL + jc * 512
                pe(lambda e, sl=sl: e.matmul(PS[1][0:4, 0:512], kgv[:, 2:6], kg_dre[:, sl],
                                             start=True, stop=False))
                pe(lambda e, sl=sl: e.matmul(PS[1][0:4, 0:512], kgv[:, 10:14], kg_dim[:, sl],
                                             start=False, stop=True))
                pe(lambda e, sl=sl: e.matmul(PS[1][0:4, 512:1024], kgv[:, 6:10], kg_dre[:, sl],
                                             start=True, stop=False))
                pe_n = pe(lambda e, sl=sl: e.matmul(PS[1][0:4, 512:1024], kgv[:, 2:6], kg_dim[:, sl],
                                                    start=False, stop=True), inc=True)
                P.wait("vector", s_pe, pe_n)
                P.wait("vector", s_kb, s_kb.n)
                dve(lambda e: e.tensor_copy(kk_small[:, 0:512], PS[1][0:4, 0:512]))
                ev_ok = dve(lambda e: e.tensor_copy(kk_small[:, 512:1024], PS[1][0:4, 512:1024]),
                            inc=True)
                P.wait("sync", s_dve, ev_ok)
                dma("sync", lambda e, gcol=gcol: e.dma_start(
                    out=bass.AP(kk_scr, gcol, [[L, 4], [1, 512]]),
                    in_=kk_small[:, 0:512]), s_kb)
                kkd = dma("sync", lambda e, gcol=gcol: e.dma_start(
                    out=bass.AP(kk_scr, 4 * L + gcol, [[L, 4], [1, 512]]),
                    in_=kk_small[:, 512:1024]), s_kb)
                P.wait("tensor", s_dve, ev_ok)
        P.wait("sync", s_kb, s_kb.n)
        for i, nm in enumerate(["k00re", "k01re", "k10re", "k11re",
                                "k00im", "k01im", "k10im", "k11im"]):
            dma("sync", lambda e, d=kg(nm), i=i: e.dma_start(
                out=d, in_=bass.AP(kk_scr, i * L, [[64, 64], [1, 64]])), s_kb)
        kvec_loaded = s_kb.n

        P.wait("vector", s_kb, kvec_loaded)
        A64 = tr2[0:64, 0:64]
        B64 = tr2[0:64, 64:128]
        C64 = tr2[0:64, 128:192]
        D64 = tr2[0:64, 192:256]
        E64 = tr2[0:64, 256:320]
        dve(lambda e: e.tensor_scalar(A64, kg("k11re"), 1.0, None, ALU.add))
        dve(lambda e: e.tensor_mul(B64, A64, A64))
        dve(lambda e: e.tensor_mul(C64, kg("k11im"), kg("k11im")))
        dve(lambda e: e.tensor_add(B64, B64, C64))
        dve(lambda e: e.reciprocal(B64, B64))
        dve(lambda e: e.tensor_mul(C64, A64, B64))
        dve(lambda e: e.scalar_tensor_tensor(D64, kg("k11im"), -1.0, B64,
                                             ALU.mult, ALU.mult))
        dve(lambda e: e.tensor_mul(A64, kg("k01re"), kg("k10re")))
        dve(lambda e: e.tensor_mul(B64, kg("k01im"), kg("k10im")))
        dve(lambda e: e.tensor_sub(A64, A64, B64))
        dve(lambda e: e.tensor_mul(B64, kg("k01re"), kg("k10im")))
        dve(lambda e: e.tensor_mul(E64, kg("k01im"), kg("k10re")))
        dve(lambda e: e.tensor_add(B64, B64, E64))
        dve(lambda e: e.tensor_mul(kg("Y_re"), A64, C64))
        dve(lambda e: e.tensor_mul(kg("Y_im"), B64, D64))
        dve(lambda e: e.tensor_sub(kg("Y_re"), kg("Y_re"), kg("Y_im")))
        dve(lambda e: e.tensor_mul(kg("Y_im"), A64, D64))
        dve(lambda e: e.tensor_mul(E64, B64, C64))
        dve(lambda e: e.tensor_add(kg("Y_im"), kg("Y_im"), E64))
        dve(lambda e: e.tensor_sub(kg("Y_re"), kg("k00re"), kg("Y_re")))
        dve(lambda e: e.tensor_sub(kg("Y_im"), kg("k00im"), kg("Y_im")))
        dve(lambda e: e.tensor_mul(kg("at_re"), kg("c_re"), kg("Y_re")))
        dve(lambda e: e.tensor_mul(E64, kg("c_im"), kg("Y_im")))
        dve(lambda e: e.tensor_sub(kg("at_re"), kg("at_re"), E64))
        dve(lambda e: e.tensor_mul(kg("at_im"), kg("c_re"), kg("Y_im")))
        dve(lambda e: e.tensor_mul(E64, kg("c_im"), kg("Y_re")))
        at_done = dve(lambda e: e.tensor_add(kg("at_im"), kg("at_im"), E64), inc=True)

        P.wait("tensor", s_dve, at_done)
        pe(lambda e: e.matmul(PS[1][0:64, 1024:1088], kg("at_re"), kg("E1_re"), start=True, stop=False))
        pe(lambda e: e.matmul(PS[1][0:64, 1024:1088], kg("at_im"), kg("E1_imn"), start=False, stop=True))
        pe(lambda e: e.matmul(PS[1][0:64, 1536:1600], kg("at_re"), kg("E1_im"), start=True, stop=False))
        pe_n = pe(lambda e: e.matmul(PS[1][0:64, 1536:1600], kg("at_im"), kg("E1_re"), start=False, stop=True), inc=True)
        P.wait("vector", s_pe, pe_n)
        dve(lambda e: e.tensor_copy(kg("Y_re"), PS[1][0:64, 1024:1088]))
        dve(lambda e: e.tensor_copy(kg("Y_im"), PS[1][0:64, 1536:1600]))
        dve(lambda e: e.tensor_mul(kg("Z_re"), kg("W_re"), kg("Y_re")))
        dve(lambda e: e.tensor_mul(E64, kg("W_im"), kg("Y_im")))
        dve(lambda e: e.tensor_sub(kg("Z_re"), kg("Z_re"), E64))
        dve(lambda e: e.tensor_mul(kg("Z_im"), kg("W_re"), kg("Y_im")))
        dve(lambda e: e.tensor_mul(E64, kg("W_im"), kg("Y_re")))
        z_done = dve(lambda e: e.tensor_add(kg("Z_im"), kg("Z_im"), E64), inc=True)
        P.wait("tensor", s_dve, z_done)
        pe(lambda e: e.matmul(PS[1][0:64, 1024:1088], kg("E2_re"), kg("Z_re"), start=True, stop=False))
        pe_n = pe(lambda e: e.matmul(PS[1][0:64, 1024:1088], kg("E2_imn"), kg("Z_im"), start=False, stop=True), inc=True)
        P.wait("scalar", s_pe, pe_n)
        kmat_done = act(lambda e: e.copy(kmat_bf[:], PS[1][0:64, 1024:1088]), inc=True)
        P.wait("sync", s_dve, zrow_done)
        dma("sync", lambda e: e.dma_start(out=bass.AP(kpad, 0, [[127, 1], [1, 127]]),
                                          in_=zrow[0:1, 0:127]), s_kp)
        P.wait("sync", s_act, kmat_done)
        dma("sync", lambda e: e.dma_start(out=bass.AP(kpad, 127, [[64, 64], [1, 64]]),
                                          in_=kmat_bf[:]), s_kp)
        P.wait("sync", s_kp, s_kp.n)
        hank_done = dma("sync", lambda e: e.dma_start(
            out=hank.ap()[:, 0:D].rearrange("q (d f) -> q d f", d=CH),
            in_=bass.AP(kpad, 0, [[1, 128], [128, CH], [1, 128]])), s_kp)
        P.wait("tensor", s_kp, hank_done)
        tt_ev = [0, 0]
        for i in range(8):
            sl = slice(i * 512, (i + 1) * 512)
            if tt_ev[i % 2]:
                P.wait("tensor", s_act, tt_ev[i % 2])
            pe_n = pe(lambda e, sl=sl, i=i: e.matmul(psv(1, i % 2), prev_sb[:], hank[:, sl],
                                                start=True, stop=True), inc=True)
            P.wait("scalar", s_pe, pe_n)
            tt_ev[i % 2] = act(lambda e, sl=sl, i=i: e.copy(ttoep[:, sl], psv(1, i % 2)), inc=True)
        ttoep_done = s_act.n
        bb_ev = [0, 0]
        for i in range(8):
            sl = slice(i * 512, (i + 1) * 512)
            if bb_ev[i % 2]:
                P.wait("tensor", s_dve, bb_ev[i % 2])
            pe_n = pe(lambda e, sl=sl, i=i: e.matmul(psv(0, i % 2), ones_sb[:],
                                                     bias_b[0:1, sl], start=True, stop=True), inc=True)
            P.wait("vector", s_pe, pe_n)
            bb_ev[i % 2] = dve(lambda e, sl=sl, i=i: e.tensor_copy(bias_b[:, sl], psv(0, i % 2)), inc=True)
        for bb in range(2):
            for g in range(4):
                set_bank(bb, g, "dve", s_dve.n)
        set_bank(1, 0, "act", ttoep_done)
        set_bank(1, 1, "act", ttoep_done)
        set_bank(1, 2, "act", kmat_done)
        set_bank(1, 3, "dve", s_dve.n)

        prepass_state = {}
        s_fst = [mksem("s_fst0"), mksem("s_fst1")]
        s_ppf = [mksem("s_ppf0"), mksem("s_ppf1")]

        def emit_prepass(upto=2 * CH):
            st = prepass_state
            if "sign_done" not in st:
                st["sign_done"] = {}
                st["ffn_win"] = [None, None]
                st["next_k"] = 0
                P.wait("gpsimd", s_ld, LD_FINAL)
            k0 = st["next_k"]
            k1 = min(upto, 2 * CH)
            st["next_k"] = max(k0, k1)
            stage_f32 = [qkv_sb[0][:].bitcast(F32)[:, 0:2048],
                         osc_bf[1][:].bitcast(F32)[:, 0:2048]]
            sign_done = st["sign_done"]
            for k in range(k0, max(k0, k1)):
                w, c = divmod(k, CH)
                if k >= 2:
                    enga, sga = sign_done[k - 2]
                    P.wait("gpsimd", s_act if enga == "act" else s_dve, sga)
                dma("gpsimd", lambda e, c=c, w=w, k=k: e.dma_start(
                    out=stage_f32[k % 2],
                    in_=wffnT.ap()[c * 128:(c + 1) * 128, w * 2048:(w + 1) * 2048]), s_ppf[k % 2])
                if k % 2 == 0:
                    P.wait("scalar", s_ppf[0], s_ppf[0].n)
                    if k >= 2:
                        P.wait("scalar", s_fst[0], (k // 2) * 16)
                    sg = act(lambda e: e.activation(
                        osc_bf[0][:, 0:2048], stage_f32[0], AF.Sign), inc=True)
                    sign_done[k] = ("act", sg)
                else:
                    P.wait("vector", s_ppf[1], s_ppf[1].n)
                    if k >= 2:
                        P.wait("vector", s_fst[1], (k // 2) * 16)
                    dve(lambda e: e.tensor_scalar(osc_bf[0][:, 2048:4096],
                                                  stage_f32[1], 0.0, None, ALU.is_gt))
                    sg = dve(lambda e: e.tensor_scalar(
                        osc_bf[0][:, 2048:4096], osc_bf[0][:, 2048:4096],
                        2.0, -1.0, ALU.mult, ALU.add), inc=True)
                    sign_done[k] = ("dve", sg)
                eng, sg = sign_done[k]
                P.wait("scalar", s_act if eng == "act" else s_dve, sg)
                dma("scalar", lambda e, c=c, w=w, k=k: e.dma_start(
                    out=ffn_bf.ap()[c * 128:(c + 1) * 128, w * 2048:(w + 1) * 2048],
                    in_=osc_bf[0][:, (k % 2) * 2048:(k % 2) * 2048 + 2048]), s_fst[k % 2])
                if c == CH - 1:
                    st["ffn_win"][w] = (s_fst[0].n, s_fst[1].n)
            if k1 < 2 * CH or "strips" in st:
                return
            st["strips"] = True
            for w in range(3):
                dma("gpsimd", lambda e, w=w: e.dma_start(
                    out=bass.AP(qkv_bf, w * 2048, [[QKVW, D], [1, 2048]]),
                    in_=bass.AP(wqkvT, w * 2048, [[QKVW, D], [1, 2048]])), s_ppq[w])
            for w in range(2):
                dma("gpsimd", lambda e, w=w: e.dma_start(
                    out=bass.AP(proj_bf, w * 2048, [[D, D], [1, 2048]]),
                    in_=bass.AP(wprojT, w * 2048, [[D, D], [1, 2048]])), s_ppp[w])

        wstate = {"n": 0, "loads": [0] * NWBUF}
        stream_gate = set()

        WQUEUE = ["sync", "sync", "sync"]

        def stream_sub(src_dram, row0, col0, ncols, gate_key):
            n = wstate["n"]
            bi = n % NWBUF
            qn = WQUEUE[bi]
            gk = (gate_key, col0 // COLW, qn)
            if gk not in stream_gate:
                w = col0 // COLW
                if gate_key == "ffn":
                    P.wait(qn, s_fst[0], prepass_state["ffn_win"][w][0])
                    P.wait(qn, s_fst[1], prepass_state["ffn_win"][w][1])
                elif gate_key == "qkv":
                    P.wait(qn, s_ppq[w], 16)
                else:
                    P.wait(qn, s_ppp[w], 16)
                stream_gate.add(gk)
            if n >= NWBUF:
                P.wait(qn, s_wfree, n - (NWBUF - 1))
            wstate["loads"][bi] += 1
            wait_val = wstate["loads"][bi] * 16
            Wfull = src_dram.shape[1]
            srcap = bass.AP(src_dram, row0 * Wfull + col0,
                            [[Wfull, 128], [128 * Wfull, CPS], [1, ncols]])
            dstap = wbuf[bi].ap()[:, 0:CPS * ncols].rearrange("p (u n) -> p u n", u=CPS)
            dma(qn, lambda e, s=srcap, d=dstap: e.dma_start(out=d, in_=s), s_w[bi])
            wstate["n"] += 1
            return bi, wait_val

        def emit_pass(src_dram, win, blocks_actT, gate_key):
            col0 = win * COLW
            for q in range(CH // CPS):
                bi, wv = stream_sub(src_dram, q * CPS * 128, col0, COLW, gate_key)
                P.wait("tensor", s_w[bi], wv)
                for cc in range(CPS):
                    c = q * CPS + cc
                    for bb in range(2):
                        for g in range(4):
                            if c == 0:
                                gate_bank(bb, g)
                            last = (cc == CPS - 1 and bb == 1 and g == 3)
                            fn = (lambda e, c=c, bb=bb, g=g, bi=bi, cc=cc:
                                  e.matmul(psv(bb, g),
                                           blocks_actT[bb][:, c * 128:(c + 1) * 128],
                                           wbuf[bi][:, cc * COLW + g * 512:cc * COLW + (g + 1) * 512],
                                           start=(c == 0), stop=(c == CH - 1)))
                            if last:
                                s_wfree.n += 1
                                P.emit("tensor", lambda e, f=fn: f(e).then_inc(s_wfree.h, 1))
                            else:
                                P.emit("tensor", fn)
            return s_wfree.n

        halves_ready = {}

        def stats_half(i, h, epi_gate):
            o = i * 12
            cols = slice(h * 2048, (h + 1) * 2048)
            P.wait("scalar", s_dve, epi_gate)
            act(lambda e, i=i, cols=cols, o=o, h=h: e.activation(
                ln_bf[:, 0:2048], res[i][:, cols], AF.Copy,
                accum_out=stat[:, o + 2 * h:o + 2 * h + 1]))
            v = act(lambda e, i=i, cols=cols, o=o, h=h: e.activation(
                ln_bf[:, 0:2048], res[i][:, cols], AF.Square,
                accum_out=stat[:, o + 2 * h + 1:o + 2 * h + 2]), inc=True)
            halves_ready.setdefault(i, {})[h] = v

        def ln_stats(i, with_rd):
            o = i * 12
            x = res[i]
            hv = halves_ready.pop(i, None)
            P.wait("scalar", s_dve, s_dve.n)
            P.wait("scalar", s_pe, lnbf_pe[0])
            P.wait("vector", s_pe, lnbf_pe[0])
            if hv is None:
                hvv = act(lambda e, o=o: e.activation(
                    ln_bf[:], x[:], AF.Square,
                    accum_out=stat[:, o + 1:o + 2]), inc=True)
                dve(lambda e, o=o: e.reduce_sum(stat[:, o:o + 1], x[:], axis=AX.X))
            else:
                hvv = max(hv.values())
                P.wait("vector", s_act, hvv)
                dve(lambda e: e.tensor_add(stat[:, o:o + 1], stat[:, o:o + 1], stat[:, o + 2:o + 3]))
                dve(lambda e: e.tensor_add(stat[:, o + 1:o + 2], stat[:, o + 1:o + 2], stat[:, o + 3:o + 4]))
            P.wait("vector", s_act, hvv)
            dve(lambda e: e.tensor_scalar(stat[:, o + 4:o + 5], stat[:, o:o + 1], 1.0 / D, None, ALU.mult))
            dve(lambda e: e.tensor_mul(stat[:, o + 9:o + 10], stat[:, o + 4:o + 5], stat[:, o + 4:o + 5]))
            v_ok = dve(lambda e: e.scalar_tensor_tensor(stat[:, o + 8:o + 9], stat[:, o + 1:o + 2],
                                                        1.0 / D, stat[:, o + 9:o + 10],
                                                        ALU.mult, ALU.subtract), inc=True)
            P.wait("scalar", s_dve, v_ok)
            act(lambda e: e.activation(stat[:, o + 9:o + 10], stat[:, o + 8:o + 9],
                                       AF.Ln, bias=epscol[:]))
            r_ok = act(lambda e: e.activation(stat[:, o + 5:o + 6], stat[:, o + 9:o + 10],
                                              AF.Exp, scale=-0.5), inc=True)
            P.wait("vector", s_act, r_ok)
            if with_rd:
                dve(lambda e: e.tensor_mul(stat[:, o + 9:o + 10], stat[:, o + 5:o + 6], dcol[:]))
                dve(lambda e: e.tensor_scalar(stat[:, o + 7:o + 8], stat[:, o + 9:o + 10],
                                              1.0, None, ALU.add))
                dve(lambda e: e.tensor_mul(stat[:, o + 6:o + 7], stat[:, o + 4:o + 5],
                                           stat[:, o + 9:o + 10]))
            outs = []
            for h in (0, 1):
                cols = slice(h * 2048, (h + 1) * 2048)
                outs.append(dve(lambda e, cols=cols, o=o: e.tensor_scalar(
                    ln_bf[:, cols], x[:, cols], stat[:, o + 4:o + 5],
                    stat[:, o + 5:o + 6], ALU.subtract, ALU.mult), inc=True))
            return tuple(outs)

        def transposes(i, src_fn, reads_lnbf=True, half_gates=None):
            evs = {}
            for c in range(CH):
                view = ps_bf(0, c % 4)
                if half_gates is not None and c % 16 == 0:
                    P.wait("tensor", s_dve, half_gates[c // 16])
                if c >= 4:
                    which, val = evs[c - 4]
                    P.wait("tensor", s_dve if which == 0 else s_act, val)
                pe_n = pe(lambda e, v=view, sap=src_fn(c): e.transpose(v, sap, ident_sb[:]),
                          inc=True)
                if c % 2 == 0:
                    P.wait("vector", s_pe, pe_n)
                    val = dve(lambda e, c=c, v=view: e.tensor_copy(
                        actT[i][:, c * 128:(c + 1) * 128], v), inc=True)
                    evs[c] = (0, val)
                else:
                    P.wait("scalar", s_pe, pe_n)
                    val = act(lambda e, c=c, v=view: e.copy(
                        actT[i][:, c * 128:(c + 1) * 128], v), inc=True)
                    evs[c] = (1, val)
            if reads_lnbf:
                lnbf_pe[0] = s_pe.n
            for g in range(4):
                wg, vg = evs[CH - 4 + g]
                set_bank(0, g, "dve" if wg == 0 else "act", vg)
            return s_dve.n, s_act.n

        def sub_conv(T, prestats=False):
            for i in (0, 1):
                hg = ln_stats(i, with_rd=True)
                gate_banks_for_pe()
                transposes(i, lambda c: ln_bf[:, c * 128:(c + 1) * 128], half_gates=hg)
            tp_d, tp_a = s_dve.n, s_act.n
            for qd in range(2):
                P.wait("tensor", s_dve, tp_d)
                P.wait("tensor", s_act, tp_a)
                gate_banks_for_pe()
                maxJ = 4 * (4 * qd + 3) + 3
                work = []
                for J in range(0, min(CH, maxJ + 1)):
                    for bb in range(2):
                        for g in range(4):
                            cgg = 4 * qd + g
                            if J > 4 * cgg + 3:
                                continue
                            if J <= 4 * cgg:
                                w, roff, poff = 512, 128 * (4 * cgg - J), 0
                            else:
                                w = 512 - 128 * (J - 4 * cgg)
                                roff, poff = 0, 128 * (J - 4 * cgg)
                            work.append((bb, g, J, w, roff, poff, 4 * cgg + 3))
                for idx, (bb, g, J, w, roff, poff, lastJ) in enumerate(work):
                    fn = (lambda e, bb=bb, g=g, J=J, w=w, roff=roff, poff=poff, lastJ=lastJ:
                          e.matmul(psv(bb, g, off=poff, w=w),
                                   actT[bb][:, J * 128:(J + 1) * 128],
                                   ttoep[:, roff:roff + w],
                                   start=(J == 0), stop=(J == lastJ)))
                    conv_pe = pe(fn, inc=(idx == len(work) - 1))
                P.wait("vector", s_pe, conv_pe)
                for bb in range(2):
                    o = bb * 12
                    for g in range(4):
                        cols = slice(qd * 2048 + g * 512, qd * 2048 + (g + 1) * 512)
                        t2 = tr2[:, g * 512:(g + 1) * 512]
                        dve(lambda e, bb=bb, cols=cols, t2=t2, o=o, g=g: e.scalar_tensor_tensor(
                            t2, res[bb][:, cols], stat[:, o + 7:o + 8], psv(bb, g),
                            ALU.mult, ALU.add))
                        v = dve(lambda e, bb=bb, cols=cols, t2=t2, o=o: e.tensor_scalar(
                            res[bb][:, cols], t2, stat[:, o + 6:o + 7], None,
                            ALU.subtract), inc=True)
                        set_bank(bb, g, "dve", v)
                    if prestats:
                        stats_half(bb, qd, s_dve.n)

        def stage1_ffn(i):
            hg = ln_stats(i, with_rd=False)
            gate_banks_for_pe()
            tp_d, tp_a = transposes(i, lambda c: ln_bf[:, c * 128:(c + 1) * 128],
                                    half_gates=hg)
            P.wait("tensor", s_dve, tp_d)
            P.wait("tensor", s_act, tp_a)
            gate_bank(1, 0)
            for c in range(CH):
                lg_pe = pe(lambda e, c=c: e.matmul(PS[1][:, 0:8],
                                           actT[i][:, c * 128:(c + 1) * 128],
                                           gate_sb[:, c * NEXP:(c + 1) * NEXP],
                                           start=(c == 0), stop=(c == CH - 1)),
                           inc=(c == CH - 1))
            P.wait("vector", s_pe, lg_pe)
            dve(lambda e: e.reduce_max(alph[:, 0:1], PS[1][:, 0:8], axis=AX.X))
            nm_ok = dve(lambda e: e.tensor_scalar(alph[:, 1:2], alph[:, 0:1], -1.0,
                                                  None, ALU.mult), inc=True)
            P.wait("scalar", s_dve, nm_ok)
            ex_ok = act(lambda e: e.activation(rout_f[:], PS[1][:, 0:8], AF.Exp,
                                               bias=alph[:, 1:2]), inc=True)
            set_bank(1, 0, "act", ex_ok)
            P.wait("vector", s_act, ex_ok)
            dve(lambda e: e.reduce_sum(alph[:, 0:1], rout_f[:], axis=AX.X))
            dve(lambda e: e.reciprocal(alph[:, 0:1], alph[:, 0:1]))
            gate_bank(1, 1)
            P.wait("vector", s_pe, last_rtp[0])
            rt_ok = dve(lambda e: e.tensor_scalar(rout_bf[:], rout_f[:], alph[:, 0:1],
                                                  None, ALU.mult), inc=True)
            P.wait("tensor", s_dve, rt_ok)
            last_rtp[0] = last_rtp[0]
            rtp = pe(lambda e: e.transpose(PS[1][:].bitcast(BF16)[0:8, 2048:2176],
                                           rout_bf[:], ident_sb[:]), inc=True)
            last_rtp[0] = rtp
            P.wait("scalar", s_pe, rtp)
            rT_ok = act(lambda e, i=i: e.copy(routT[i][:], PS[1][:].bitcast(BF16)[0:8, 2048:2176]), inc=True)
            set_bank(1, 1, "act", rT_ok)
            P.wait("tensor", s_act, rT_ok)
            INSLOT = [(0, 2), (1, 2), (0, 3), (1, 3)]
            ins_mults = [0, 0, 0, 0]
            for sl in INSLOT:
                gate_bank(*sl)
            for grp in range(8):
                pb, gsl = INSLOT[grp % 4]
                if ins_mults[grp % 4]:
                    P.wait("tensor", s_dve, ins_mults[grp % 4])
                for cc in range(4):
                    c = grp * 4 + cc
                    g_pe = pe(lambda e, c=c, cc=cc, i=i, pb=pb, gsl=gsl: e.matmul(
                        psv(pb, gsl, off=cc * 128, w=128),
                        ics_sb[:, c * 128:(c + 1) * 128], routT[i][:],
                        start=True, stop=True), inc=(cc == 3))
                P.wait("vector", s_pe, g_pe)
                ins_mults[grp % 4] = dve(lambda e, grp=grp, i=i, pb=pb, gsl=gsl: e.tensor_mul(
                    actT[i][:, grp * 512:(grp + 1) * 512],
                    actT[i][:, grp * 512:(grp + 1) * 512],
                    psv(pb, gsl)), inc=True)
                set_bank(pb, gsl, "dve", ins_mults[grp % 4])
            zT_d = s_dve.n
            osc_ev = [0, 0]
            gate_bank(0, 3)
            gate_bank(1, 3)
            if not osc_gate[0]:
                P.wait("scalar", s_fst[0], prepass_state["ffn_win"][1][0])
                P.wait("scalar", s_fst[1], prepass_state["ffn_win"][1][1])
                osc_gate[0] = True
            for gg in range(8):
                pb = gg % 2
                if osc_ev[pb]:
                    P.wait("tensor", s_act, osc_ev[pb])
                po = pe(lambda e, gg=gg, i=i, pb=pb: e.matmul(
                    psv(pb, 3), routT[i][:], ocs_sb[:, gg * 512:(gg + 1) * 512],
                    start=True, stop=True), inc=True)
                P.wait("scalar", s_pe, po)
                osc_ev[pb] = act(lambda e, gg=gg, i=i, pb=pb: e.copy(
                    osc_bf[i][:, gg * 512:(gg + 1) * 512], psv(pb, 3)), inc=True)
                set_bank(pb, 3, "act", osc_ev[pb])
            return zT_d, s_act.n

        def sub_ffn(T, prestats=False):
            epi_b = [0, 0]
            for i in (0, 1):
                stage1_ffn(i)
            for bb in range(2):
                dve(lambda e, bb=bb: e.tensor_add(res[bb][:], res[bb][:], bias_b[:]))
            for win in range(2):
                wf = emit_pass(ffn_bf, win, actT, "ffn")
                P.wait("vector", s_wfree, wf)
                for bb in range(2):
                    for g in range(4):
                        cols = slice(win * 2048 + g * 512, win * 2048 + (g + 1) * 512)
                        t2 = tr2[:, g * 512:(g + 1) * 512]
                        dve(lambda e, bb=bb, g=g, cols=cols, t2=t2: e.tensor_mul(
                            t2, psv(bb, g), osc_bf[bb][:, cols]))
                        v = dve(lambda e, bb=bb, cols=cols, t2=t2: e.tensor_add(
                            res[bb][:, cols], res[bb][:, cols], t2), inc=True)
                        set_bank(bb, g, "dve", v)
                    if prestats:
                        stats_half(bb, win, s_dve.n)
                    epi_b[bb] = s_dve.n
            return epi_b

        def sub_attn(T, prestats=False):
            for i in (0, 1):
                hg = ln_stats(i, with_rd=False)
                gate_banks_for_pe()
                transposes(i, lambda c: ln_bf[:, c * 128:(c + 1) * 128], half_gates=hg)
            tp_d, tp_a = s_dve.n, s_act.n
            P.wait("tensor", s_dve, tp_d)
            P.wait("tensor", s_act, tp_a)
            kviewS = lambda b, k0, nk: bass.AP(qkv_sb[b].ap().tensor, k0 * 768 + 512,
                                               [[QKVW, 128], [768, nk], [1, 128]])
            vviewS = lambda b, k0, nk: bass.AP(qkv_sb[b].ap().tensor, k0 * 768 + 640,
                                               [[QKVW, 128], [768, nk], [1, 128]])
            qviewS = lambda b, g, k0, nk: bass.AP(qkv_sb[b].ap().tensor, k0 * 768 + g * 128,
                                                  [[QKVW, 128], [768, nk], [1, 128]])
            s_at = s_atn

            def poolop(fn):
                P.wait("gpsimd", s_at, s_at.n)
                s_at.n += 1
                P.emit("gpsimd", lambda e, fn=fn: fn(e).then_inc(s_at.h, 1))
                return s_at.n

            SUBSETS = {0: (0, 2), 1: (2, 3), 2: (5, 3)}
            out0_ok = 0
            for win in range(3):
                wf = emit_pass(qkv_bf, win, actT, "qkv")
                P.wait("scalar", s_wfree, wf)
                for bb in range(2):
                    for g in range(4):
                        cols = slice(win * 2048 + g * 512, win * 2048 + (g + 1) * 512)
                        v = act(lambda e, bb=bb, g=g, cols=cols: e.copy(
                            qkv_sb[bb][:, cols], psv(bb, g)), inc=True)
                        set_bank(bb, g, "act", v)
                qkv_ev = s_act.n
                k0, nk = SUBSETS[win]
                koff = k0 * 128
                P.wait("gpsimd", s_act, qkv_ev)
                poolop(lambda e, k0=k0, nk=nk, koff=koff: e.tensor_sub(
                    dkbuf[:, koff:koff + nk * 128], kviewS(1, k0, nk), kviewS(0, k0, nk)))
                poolop(lambda e, k0=k0, nk=nk, koff=koff: e.tensor_sub(
                    dvbuf[:, koff:koff + nk * 128], vviewS(1, k0, nk), vviewS(0, k0, nk)))
                for g in range(4):
                    pk = poolop(lambda e, g=g, k0=k0, nk=nk, koff=koff: e.tensor_mul(
                        prodb[:, 0:nk * 128], qviewS(1, g, k0, nk),
                        dkbuf[:, koff:koff + nk * 128]))
                    P.wait("vector", s_at, pk)
                    dve(lambda e, g=g, k0=k0, nk=nk: e.tensor_reduce(
                        bass.AP(alph.ap().tensor, 4 * k0 + g, [[64, 128], [4, nk]]),
                        prodb.ap()[:, 0:nk * 128].rearrange("p (k d) -> p k d", k=nk),
                        axis=AX.X, op=ALU.add), inc=True)
                    if g < 3:
                        P.wait("gpsimd", s_dve, s_dve.n)
                sc_ok = s_dve.n
                P.wait("scalar", s_dve, sc_ok)
                acols = slice(32 + 4 * k0, 32 + 4 * (k0 + nk))
                scols = slice(4 * k0, 4 * (k0 + nk))
                al_ok = act(lambda e, acols=acols, scols=scols: e.activation(
                    alph[:, acols], alph[:, scols], AF.Exp, scale=-ATT_SCALE), inc=True)
                P.wait("vector", s_act, al_ok)
                dve(lambda e, acols=acols: e.tensor_scalar(alph[:, acols], alph[:, acols],
                                                           1.0, None, ALU.add))
                sig_ok = dve(lambda e, acols=acols: e.reciprocal(alph[:, acols], alph[:, acols]),
                             inc=True)
                for g in range(4):
                    out0_ok = act(lambda e, g=g, k0=k0, nk=nk: e.copy(
                        qviewS(0, g, k0, nk), vviewS(0, k0, nk)), inc=True)
                P.wait("gpsimd", s_dve, sig_ok)
                for g in range(4):
                    avw = bass.AP(alph.ap().tensor, 32 + 4 * k0 + g, [[64, 128], [4, nk], [0, 128]])
                    poolop(lambda e, g=g, avw=avw, k0=k0, nk=nk, koff=koff: e.tensor_mul(
                        prodb[:, 0:nk * 128], dvbuf[:, koff:koff + nk * 128], avw))
                    poolop(lambda e, g=g, k0=k0, nk=nk: e.tensor_add(
                        qviewS(1, g, k0, nk), prodb[:, 0:nk * 128], vviewS(0, k0, nk)))
            at_ok = s_at.n
            P.wait("tensor", s_at, at_ok)
            P.wait("tensor", s_act, out0_ok)
            for i in (0, 1):
                P.wait("tensor", s_atn, at_ok)
                gate_banks_for_pe()
                transposes(i, lambda c, i=i: qkv_sb[i][:, (c // 4) * 768 + (c % 4) * 128:(c // 4) * 768 + (c % 4) * 128 + 128], reads_lnbf=False)
            tp2_d, tp2_a = s_dve.n, s_act.n
            P.wait("tensor", s_dve, tp2_d)
            P.wait("tensor", s_act, tp2_a)
            for win in range(2):
                wf = emit_pass(proj_bf, win, actT, "proj")
                P.wait("vector", s_wfree, wf)
                for bb in range(2):
                    for g in range(4):
                        cols = slice(win * 2048 + g * 512, win * 2048 + (g + 1) * 512)
                        v = dve(lambda e, bb=bb, g=g, cols=cols: e.tensor_add(
                            res[bb][:, cols], res[bb][:, cols], psv(bb, g)), inc=True)
                        set_bank(bb, g, "dve", v)
                    if prestats:
                        stats_half(bb, win, s_dve.n)

        for T in range(NT):
            if T > 0:
                for i in (0, 1):
                    s, r0 = (0, T * 128) if i == 0 else (1, T * 128)
                    P.wait("gpsimd", s_st[i], 16 * T)
                    dma("gpsimd", lambda e, i=i, s=s, r0=r0: e.dma_start(
                        out=res[i][:], in_=x_in.ap()[s, r0:r0 + 128, :]), s_x)
            x_ready = s_x.n
            for eng in ("vector", "scalar"):
                P.wait(eng, s_x, x_ready)
            if SUBLAYERS[0]:
                sub_conv(T, prestats=SUBLAYERS[1])
            if T == 0:
                emit_prepass()
            if SUBLAYERS[1]:
                sub_ffn(T, prestats=SUBLAYERS[2])
            if SUBLAYERS[2]:
                sub_attn(T, prestats=SUBLAYERS[3])
            if SUBLAYERS[3]:
                epi_b = sub_ffn(T)
            else:
                epi_b = [s_dve.n, s_dve.n]
            for i in (0, 1):
                s, r0 = (0, T * 128) if i == 0 else (1, T * 128)
                P.wait("gpsimd", s_dve, epi_b[i])
                dma("gpsimd", lambda e, i=i, s=s, r0=r0: e.dma_start(
                    out=x_out.ap()[s, r0:r0 + 128, :], in_=res[i][:]), s_st[i])

        P.wait("sync", s_st[0], s_st[0].n)
        P.wait("sync", s_st[1], s_st[1].n)

        with nc.Block() as block:
            P.replay(block)

    return nc


def kernel(**inputs):
    consts = _host_consts()
    u = np.ascontiguousarray(inputs["L0_attn_a_u"])
    lam = inputs["L0_attn_a_Lambda_ri"]
    p_ri = inputs["L0_attn_a_p_ri"]
    q_ri = inputs["L0_attn_a_q_ri"]
    vc = inputs["L0_attn_a_Vc_ri"]
    ct = inputs["L0_attn_a_Ct_ri"]
    f32c = lambda a: np.ascontiguousarray(np.asarray(a), dtype=np.float32)
    base = {
        "wqkvT": f32c(np.asarray(inputs["L0_attn_b_weight_qkv"]).T),
        "wprojT": f32c(np.asarray(inputs["L0_attn_b_weight_proj"]).T),
        "wffnT": f32c(np.asarray(inputs["L0_ffn_weight"]).T),
        "gateT": f32c(np.asarray(inputs["L0_ffn_gate_w"]).T),
        "ics": f32c(inputs["L0_ffn_in_channel_scale"]),
        "ocs": f32c(inputs["L0_ffn_out_channel_scale"]),
        "bias_in": f32c(np.asarray(inputs["L0_ffn_bias"]).reshape(1, -1)),
        "lam_re": f32c(np.asarray(lam)[:, 0].reshape(64, 1)),
        "lam_im": f32c(np.asarray(lam)[:, 1].reshape(64, 1)),
        "p_re": f32c(np.asarray(p_ri)[:, 0].reshape(64, 1)),
        "p_im": f32c(np.asarray(p_ri)[:, 1].reshape(64, 1)),
        "q_re": f32c(np.asarray(q_ri)[:, 0].reshape(64, 1)),
        "q_im": f32c(np.asarray(q_ri)[:, 1].reshape(64, 1)),
        "ct_re": f32c(np.asarray(ct)[0, :, 0].reshape(64, 1)),
        "ct_im": f32c(np.asarray(ct)[0, :, 1].reshape(64, 1)),
        "vcT_re": f32c(np.asarray(vc)[:, :, 0].T),
        "vcT_im": f32c(np.asarray(vc)[:, :, 1].T),
        "bp_in": f32c(np.asarray(inputs["L0_attn_a_Bp"]).reshape(64, 1)),
        "d_in": f32c(np.asarray(inputs["L0_attn_a_D"]).reshape(1, 1)),
        "ls_in": f32c(np.asarray(inputs["L0_attn_a_log_step"]).reshape(1, 1)),
    }
    base.update(consts)

    nc = build_program()
    in_maps = []
    for c in range(NCORE):
        m = dict(base)
        m["x_in"] = np.ascontiguousarray(u[:, c * 1024:(c + 1) * 1024, :])
        in_maps.append(m)
    res = run_bass_kernel_spmd(nc, in_maps, core_ids=list(range(NCORE)))
    out = np.concatenate([r["x_out"] for r in res.results], axis=1)
    return np.ascontiguousarray(out, dtype=np.float32)


# version 19 (final)
# speedup vs baseline: 1.0645x; 1.0645x over previous
import numpy as np
import ml_dtypes
import concourse.bass as bass
import concourse.mybir as mybir
from concourse.bass_utils import run_bass_kernel_spmd
from contextlib import ExitStack

F32 = mybir.dt.float32
BF16 = mybir.dt.bfloat16
AF = mybir.ActivationFunctionType
ALU = mybir.AluOpType
AX = mybir.AxisListType

D = 4096
PB = 128
CH = D // PB
NCORE = 8
NT = 8
QKVW = 6144
NEXP = 8
L = 4096
EPS = 1e-5
ATT_SCALE = float(1.0 / np.sqrt(128.0))
CPS = 4
COLW = 2048
NWBUF = 3

SUBLAYERS = (True, True, True, True)

ENGINES = ("sync", "gpsimd", "scalar", "vector", "tensor")


class Sem:
    def __init__(self, handle, name):
        self.h = handle
        self.name = name
        self.n = 0


class Prog:
    def __init__(self):
        self.ops = {e: [] for e in ENGINES}
        self.waited = {e: {} for e in ENGINES}

    def emit(self, eng, fn):
        self.ops[eng].append(fn)

    def wait(self, eng, sem, val):
        if val > 0 and self.waited[eng].get(sem.name, -1) < val:
            self.waited[eng][sem.name] = val
            self.ops[eng].append(lambda e, s=sem.h, v=val: e.wait_ge(s, v))

    def replay(self, block):
        for eng in ENGINES:
            ops = self.ops[eng]
            getattr(block, eng)(lambda e, ops=ops: [op(e) for op in ops])


def _host_consts():
    j = np.arange(L)
    t32 = (j / L).astype(np.float32)
    Om = np.exp(np.complex64(-2j * np.pi) * t32).astype(np.complex64)
    om = ((1 - Om) / (1 + Om)).astype(np.complex64)
    cc = (np.complex64(2.0) / (1 + Om)).astype(np.complex64)

    i64 = np.arange(64)
    E1 = np.exp(2j * np.pi * np.outer(i64, i64) / 64.0)
    Wt = np.exp(2j * np.pi * np.outer(i64, i64) / 4096.0)
    E2s = E1 / 4096.0

    f32 = lambda a: np.ascontiguousarray(a, dtype=np.float32)
    return {
        "om_re": f32(np.broadcast_to(om.real, (64, L))),
        "om_im": f32(np.broadcast_to(om.imag, (64, L))),
        "c_re": f32(cc.real.reshape(64, 64)),
        "c_im": f32(cc.imag.reshape(64, 64)),
        "E1_re": f32(E1.real), "E1_im": f32(E1.imag), "E1_imn": f32(-E1.imag),
        "W_re": f32(Wt.real), "W_im": f32(Wt.imag),
        "E2_re": f32(E2s.real), "E2_imn": f32(-E2s.imag),
        "ident_bf": np.eye(128, dtype=ml_dtypes.bfloat16),
        "prev_bf": np.eye(128, dtype=ml_dtypes.bfloat16)[::-1].copy(),
        "ones_f": np.ones((1, 128), dtype=np.float32),
    }


def build_program():
    nc = bass.Bass("TRN2", target_bir_lowering=False)

    x_in = nc.dram_tensor("x_in", [2, 1024, D], F32, kind="ExternalInput")
    x_out = nc.dram_tensor("x_out", [2, 1024, D], F32, kind="ExternalOutput")
    wqkvT = nc.dram_tensor("wqkvT", [D, QKVW], F32, kind="ExternalInput")
    wprojT = nc.dram_tensor("wprojT", [D, D], F32, kind="ExternalInput")
    wffnT = nc.dram_tensor("wffnT", [D, D], F32, kind="ExternalInput")
    gateT = nc.dram_tensor("gateT", [D, NEXP], F32, kind="ExternalInput")
    ics = nc.dram_tensor("ics", [NEXP, D], F32, kind="ExternalInput")
    ocs = nc.dram_tensor("ocs", [NEXP, D], F32, kind="ExternalInput")
    bias_in = nc.dram_tensor("bias_in", [1, D], F32, kind="ExternalInput")
    s4_small = {}
    for nm in ["lam_re", "lam_im", "p_re", "p_im", "q_re", "q_im",
               "ct_re", "ct_im", "bp_in"]:
        s4_small[nm] = nc.dram_tensor(nm, [64, 1], F32, kind="ExternalInput")
    vcT_re = nc.dram_tensor("vcT_re", [64, 64], F32, kind="ExternalInput")
    vcT_im = nc.dram_tensor("vcT_im", [64, 64], F32, kind="ExternalInput")
    d_in = nc.dram_tensor("d_in", [1, 1], F32, kind="ExternalInput")
    ls_in = nc.dram_tensor("ls_in", [1, 1], F32, kind="ExternalInput")
    cn = {}
    for name, shp, dt in [
        ("om_re", [64, L], F32), ("om_im", [64, L], F32),
        ("c_re", [64, 64], F32), ("c_im", [64, 64], F32),
        ("E1_re", [64, 64], F32), ("E1_im", [64, 64], F32), ("E1_imn", [64, 64], F32),
        ("W_re", [64, 64], F32), ("W_im", [64, 64], F32),
        ("E2_re", [64, 64], F32), ("E2_imn", [64, 64], F32),
        ("ident_bf", [128, 128], BF16), ("prev_bf", [128, 128], BF16),
        ("ones_f", [1, 128], F32),
    ]:
        cn[name] = nc.dram_tensor(name, shp, dt, kind="ExternalInput")
    qkv_bf = nc.dram_tensor("qkv_bf", [D, QKVW], BF16, kind="Internal")
    proj_bf = nc.dram_tensor("proj_bf", [D, D], BF16, kind="Internal")
    ffn_bf = nc.dram_tensor("ffn_bf", [D, D], BF16, kind="Internal")
    kk_scr = nc.dram_tensor("kk_scr", [8, L], F32, kind="Internal")
    kpad = nc.dram_tensor("kpad", [127 + L], BF16, kind="Internal")

    P = Prog()

    with ExitStack() as stack:
        def sb(name, shape, dt):
            return stack.enter_context(nc.sbuf_tensor(name, shape, dt))

        def mksem(name):
            return Sem(stack.enter_context(nc.semaphore(name)), name)

        res = [sb(f"res{b}", [PB, D], F32) for b in range(2)]
        actT = [sb(f"actT{b}", [PB, D], BF16) for b in range(2)]
        wbuf = [sb(f"wbuf{i}", [PB, CPS * COLW], BF16) for i in range(NWBUF)]
        bias_b = sb("bias_b", [PB, D], F32)
        ttoep = sb("ttoep", [PB, D], BF16)
        qkv_sb = [sb(f"qkvsb{b}", [PB, QKVW], BF16) for b in range(2)]
        osc_bf = [sb(f"oscbf{b}", [PB, D], BF16) for b in range(2)]
        ln_bf = sb("ln_bf", [PB, D], BF16)
        tr2 = sb("tr2", [PB, D // 2], F32)
        prodb = sb("prodb", [PB, 1024], F32)
        gate_sb = sb("gate_sb", [PB, CH * NEXP], BF16)
        ics_sb = sb("ics_sb", [NEXP, D], BF16)
        ocs_sb = sb("ocs_sb", [NEXP, D], BF16)
        ident_sb = sb("ident_sb", [128, 128], BF16)
        prev_sb = sb("prev_sb", [128, 128], BF16)
        ones_sb = sb("ones_sb", [1, 128], F32)
        routT = [sb(f"routT{b}", [NEXP, PB], BF16) for b in range(2)]
        rout_f = sb("rout_f", [PB, NEXP], F32)
        rout_bf = sb("rout_bf", [PB, NEXP], BF16)
        stat = sb("stat", [PB, 24], F32)
        epscol = sb("epscol", [PB, 1], F32)
        dcol = sb("dcol", [PB, 1], F32)
        alph = sb("alph", [PB, 64], F32)
        kg_a2 = sb("kg_a2", [64, 1], F32)
        kg_dre = tr2[0:64, 1024:2048]
        kg_dim = prodb[0:64, 0:1024]
        kg_t = qkv_sb[1][:].bitcast(F32)[0:64, 2048:3072]
        kk_small = tr2[0:4, 0:1024]
        kgv = sb("kgv", [64, 24], F32)
        HL = L // 4
        kg64 = sb("kg64", [64, 64 * 26], F32)
        kmat_bf = sb("kmat_bf", [64, 64], BF16)
        zrow = sb("zrow", [1, 128], BF16)
        s4c_sb = sb("s4c_sb", [64, 12], F32)
        misc1 = sb("misc1", [1, 8], F32)
        hank = qkv_sb[1]
        dkbuf = tr2[:, 0:1024]
        dvbuf = tr2[:, 1024:2048]

        PS = [stack.enter_context(nc.psum_tensor(f"PS{bb}", [PB, 2048], F32))
              for bb in range(2)]

        def psv(bb, g, off=0, w=512):
            return PS[bb][:, g * 512 + off:g * 512 + off + w]

        def ps_bf(bb, g, cols=128, off=0):
            return PS[bb][:].bitcast(BF16)[:, g * 1024 + off:g * 1024 + off + cols]

        s_x = mksem("s_x")
        s_w = [mksem(f"s_w{i}") for i in range(NWBUF)]
        s_wfree = mksem("s_wfree")
        s_pe = mksem("s_pe")
        s_dve = mksem("s_dve")
        s_act = mksem("s_act")
        s_pool = mksem("s_pool")
        s_ppq = [mksem(f"s_ppq{i}") for i in range(3)]
        s_ppp = [mksem(f"s_ppp{i}") for i in range(2)]
        s_ldp = mksem("s_ldp")
        s_st = [mksem("s_st0"), mksem("s_st1")]
        s_ld = mksem("s_ld")
        s_kb = mksem("s_kb")
        s_kp = mksem("s_kp")
        s_atn = mksem("s_atn")

        def pe(fn, inc=False):
            if inc:
                s_pe.n += 1
                P.emit("tensor", lambda e, fn=fn: fn(e).then_inc(s_pe.h, 1))
            else:
                P.emit("tensor", fn)
            return s_pe.n

        def dve(fn, inc=True):
            P.wait("vector", s_dve, s_dve.n)
            s_dve.n += 1
            P.emit("vector", lambda e, fn=fn: fn(e).then_inc(s_dve.h, 1))
            return s_dve.n

        def act(fn, inc=True):
            P.wait("scalar", s_act, s_act.n)
            s_act.n += 1
            P.emit("scalar", lambda e, fn=fn: fn(e).then_inc(s_act.h, 1))
            return s_act.n

        def dma(eng, fn, sem):
            sem.n += 16
            P.emit(eng, lambda e, fn=fn: fn(e).then_inc(sem.h, 16))
            return sem.n

        lnbf_pe = [0]
        last_rtp = [0]
        osc_gate = [False]

        bank_free = {(bb, g): ("dve", 0) for bb in range(2) for g in range(4)}

        def set_bank(bb, g, eng, val):
            bank_free[(bb, g)] = (eng, val)

        def gate_bank(bb, g):
            eng, val = bank_free[(bb, g)]
            P.wait("tensor", s_dve if eng == "dve" else s_act, val)

        def gate_banks_for_pe():
            for bb in range(2):
                for g in range(4):
                    gate_bank(bb, g)

        dma("sync", lambda e: e.dma_start(out=ident_sb[:], in_=cn["ident_bf"].ap()), s_ld)
        dma("sync", lambda e: e.dma_start(out=prev_sb[:], in_=cn["prev_bf"].ap()), s_ld)
        dma("sync", lambda e: e.dma_start(out=ones_sb[:], in_=cn["ones_f"].ap()), s_ld)
        dma("sync", lambda e: e.dma_start(out=bias_b[0:1, :], in_=bias_in.ap()), s_ld)

        KG = {}
        _slot = [0]

        def kg_new(name):
            KG[name] = _slot[0]
            _slot[0] += 1
            return kg64[:, KG[name] * 64:(KG[name] + 1) * 64]

        def kg(name):
            return kg64[:, KG[name] * 64:(KG[name] + 1) * 64]

        for name in ["c_re", "c_im", "E1_re", "E1_im", "E1_imn", "W_re", "W_im",
                     "E2_re", "E2_imn"]:
            dst = kg_new(name)
            dma("sync", lambda e, d=dst, s=cn[name]: e.dma_start(out=d, in_=s.ap()), s_ld)
        dst = kg_new("vc_re")
        dma("sync", lambda e, d=dst: e.dma_start(out=d, in_=vcT_re.ap()), s_ld)
        dst = kg_new("vc_im")
        dma("sync", lambda e, d=dst: e.dma_start(out=d, in_=vcT_im.ap()), s_ld)
        for nm in ["at_re", "at_im", "Y_re", "Y_im", "Z_re", "Z_im",
                   "k00re", "k01re", "k10re", "k11re", "k00im", "k01im",
                   "k10im", "k11im"]:
            kg_new(nm)

        S4C = {}
        for i, nm in enumerate(["lam_re", "lam_im", "p_re", "p_im", "q_re",
                                "q_im", "ct_re", "ct_im", "bp_in"]):
            S4C[nm] = i
            dma("sync", lambda e, i=i, t=s4_small[nm]: e.dma_start(
                out=s4c_sb[:, i:i + 1], in_=t.ap()), s_ld)

        def s4c(nm):
            return s4c_sb[:, S4C[nm]:S4C[nm] + 1]

        dma("sync", lambda e: e.dma_start(out=misc1[0:1, 0:1], in_=d_in.ap()), s_ld)
        dma("sync", lambda e: e.dma_start(out=misc1[0:1, 1:2], in_=ls_in.ap()), s_ld)
        dma("gpsimd", lambda e: e.dma_start(
            out=gate_sb.ap().rearrange("p (c x) -> p c x", c=CH),
            in_=bass.AP(gateT, 0, [[NEXP, 128], [128 * NEXP, CH], [1, NEXP]])), s_ldp)
        dma("gpsimd", lambda e: e.dma_start(out=ics_sb[:], in_=ics.ap()), s_ldp)
        dma("gpsimd", lambda e: e.dma_start(out=ocs_sb[:], in_=ocs.ap()), s_ldp)
        LD_FINAL = s_ld.n
        LD_POOL = s_ldp.n
        for eng in ("vector", "scalar", "tensor"):
            P.wait(eng, s_ld, LD_FINAL)
            P.wait(eng, s_ldp, LD_POOL)

        dma("gpsimd", lambda e: e.dma_start(out=res[0][:], in_=x_in.ap()[0, 0:128, :]), s_x)
        dma("gpsimd", lambda e: e.dma_start(out=res[1][:], in_=x_in.ap()[1, 0:128, :]), s_x)

        dve(lambda e: e.memset(epscol[:], EPS))
        zrow_done = dve(lambda e: e.memset(zrow[:], 0.0), inc=True)

        pe_n = pe(lambda e: e.matmul(PS[0][:, 0:1], ones_sb[:], misc1[0:1, 0:1],
                                     start=True, stop=True), inc=True)
        P.wait("vector", s_pe, pe_n)
        dve(lambda e: e.tensor_copy(dcol[:], PS[0][:, 0:1]))

        act(lambda e: e.activation(misc1[0:1, 2:3], misc1[0:1, 1:2], AF.Exp, scale=-1.0))
        a2_done = act(lambda e: e.mul(misc1[0:1, 3:4], misc1[0:1, 2:3], 2.0), inc=True)
        P.wait("tensor", s_act, a2_done)
        pe_n = pe(lambda e: e.matmul(PS[0][0:64, 512:513], ones_sb[:, 0:64],
                                     misc1[0:1, 3:4], start=True, stop=True), inc=True)
        P.wait("vector", s_pe, pe_n)
        a2col_done = dve(lambda e: e.tensor_copy(kg_a2[:], PS[0][0:64, 512:513]), inc=True)

        pe(lambda e: e.matmul(PS[0][0:64, 1024:1025], kg("vc_re"), s4c("bp_in"),
                              start=True, stop=True))
        pe_n = pe(lambda e: e.matmul(PS[0][0:64, 1536:1537], kg("vc_im"), s4c("bp_in"),
                                     start=True, stop=True), inc=True)
        P.wait("vector", s_pe, pe_n)
        dve(lambda e: e.tensor_copy(kgv[:, 0:1], PS[0][0:64, 1024:1025]))
        dve(lambda e: e.tensor_copy(kgv[:, 1:2], PS[0][0:64, 1536:1537]))

        def cmul_conj(dr, di, are, aim, bre, bim):
            dve(lambda e: e.tensor_mul(kgv[:, 20:21], are, bre))
            dve(lambda e: e.tensor_mul(dr, aim, bim))
            dve(lambda e: e.tensor_add(dr, dr, kgv[:, 20:21]))
            dve(lambda e: e.tensor_mul(kgv[:, 20:21], are, bim))
            dve(lambda e: e.tensor_mul(di, aim, bre))
            dve(lambda e: e.tensor_sub(di, kgv[:, 20:21], di))

        cmul_conj(kgv[:, 2:3], kgv[:, 6:7], s4c("ct_re"), s4c("ct_im"), kgv[:, 0:1], kgv[:, 1:2])
        cmul_conj(kgv[:, 3:4], kgv[:, 7:8], s4c("ct_re"), s4c("ct_im"), s4c("p_re"), s4c("p_im"))
        cmul_conj(kgv[:, 4:5], kgv[:, 8:9], s4c("q_re"), s4c("q_im"), kgv[:, 0:1], kgv[:, 1:2])
        cmul_conj(kgv[:, 5:6], kgv[:, 9:10], s4c("q_re"), s4c("q_im"), s4c("p_re"), s4c("p_im"))
        vv_ok = dve(lambda e: e.tensor_scalar(kgv[:, 10:14], kgv[:, 6:10], -1.0, None, ALU.mult),
                    inc=True)
        t2v = tr2[0:64, 0:HL]
        for hf in range(4):
            if hf == 0:
                P.wait("sync", s_dve, vv_ok)
            else:
                P.wait("sync", s_dve, s_dve.n)
            ld0 = dma("sync", lambda e, hf=hf: e.dma_start(
                out=kg_dre[:], in_=cn["om_re"].ap()[:, hf * HL:(hf + 1) * HL]), s_kb)
            ld1 = dma("sync", lambda e, hf=hf: e.dma_start(
                out=kg_dim[:], in_=cn["om_im"].ap()[:, hf * HL:(hf + 1) * HL]), s_kb)
            P.wait("vector", s_kb, ld1)
            dve(lambda e: e.tensor_scalar(kg_dre[:], kg_dre[:], kg_a2[:], s4c("lam_re"),
                                          ALU.mult, ALU.subtract))
            dn_dve = dve(lambda e: e.tensor_scalar(kg_dim[:], kg_dim[:], kg_a2[:], s4c("lam_im"),
                                                   ALU.mult, ALU.subtract), inc=True)
            P.wait("scalar", s_dve, dn_dve)
            sq_done = act(lambda e: e.activation(kg_t[:], kg_dre[:], AF.Square), inc=True)
            P.wait("vector", s_act, sq_done)
            dve(lambda e: e.tensor_mul(t2v, kg_dim[:], kg_dim[:]))
            dve(lambda e: e.tensor_add(kg_t[:], kg_t[:], t2v))
            dve(lambda e: e.reciprocal(kg_t[:], kg_t[:]))
            dve(lambda e: e.tensor_mul(kg_dre[:], kg_dre[:], kg_t[:]))
            inv_done = dve(lambda e: e.scalar_tensor_tensor(kg_dim[:], kg_dim[:], -1.0, kg_t[:],
                                                            ALU.mult, ALU.mult), inc=True)
            P.wait("tensor", s_dve, inv_done)
            for jc in range(2):
                sl = slice(jc * 512, (jc + 1) * 512)
                gcol = hf * HL + jc * 512
                pe(lambda e, sl=sl: e.matmul(PS[1][0:4, 0:512], kgv[:, 2:6], kg_dre[:, sl],
                                             start=True, stop=False))
                pe(lambda e, sl=sl: e.matmul(PS[1][0:4, 0:512], kgv[:, 10:14], kg_dim[:, sl],
                                             start=False, stop=True))
                pe(lambda e, sl=sl: e.matmul(PS[1][0:4, 512:1024], kgv[:, 6:10], kg_dre[:, sl],
                                             start=True, stop=False))
                pe_n = pe(lambda e, sl=sl: e.matmul(PS[1][0:4, 512:1024], kgv[:, 2:6], kg_dim[:, sl],
                                                    start=False, stop=True), inc=True)
                P.wait("vector", s_pe, pe_n)
                P.wait("vector", s_kb, s_kb.n)
                dve(lambda e: e.tensor_copy(kk_small[:, 0:512], PS[1][0:4, 0:512]))
                ev_ok = dve(lambda e: e.tensor_copy(kk_small[:, 512:1024], PS[1][0:4, 512:1024]),
                            inc=True)
                P.wait("sync", s_dve, ev_ok)
                dma("sync", lambda e, gcol=gcol: e.dma_start(
                    out=bass.AP(kk_scr, gcol, [[L, 4], [1, 512]]),
                    in_=kk_small[:, 0:512]), s_kb)
                kkd = dma("sync", lambda e, gcol=gcol: e.dma_start(
                    out=bass.AP(kk_scr, 4 * L + gcol, [[L, 4], [1, 512]]),
                    in_=kk_small[:, 512:1024]), s_kb)
                P.wait("tensor", s_dve, ev_ok)
        P.wait("sync", s_kb, s_kb.n)
        for i, nm in enumerate(["k00re", "k01re", "k10re", "k11re",
                                "k00im", "k01im", "k10im", "k11im"]):
            dma("sync", lambda e, d=kg(nm), i=i: e.dma_start(
                out=d, in_=bass.AP(kk_scr, i * L, [[64, 64], [1, 64]])), s_kb)
        kvec_loaded = s_kb.n

        P.wait("vector", s_kb, kvec_loaded)
        A64 = tr2[0:64, 0:64]
        B64 = tr2[0:64, 64:128]
        C64 = tr2[0:64, 128:192]
        D64 = tr2[0:64, 192:256]
        E64 = tr2[0:64, 256:320]
        dve(lambda e: e.tensor_scalar(A64, kg("k11re"), 1.0, None, ALU.add))
        dve(lambda e: e.tensor_mul(B64, A64, A64))
        dve(lambda e: e.tensor_mul(C64, kg("k11im"), kg("k11im")))
        dve(lambda e: e.tensor_add(B64, B64, C64))
        dve(lambda e: e.reciprocal(B64, B64))
        dve(lambda e: e.tensor_mul(C64, A64, B64))
        dve(lambda e: e.scalar_tensor_tensor(D64, kg("k11im"), -1.0, B64,
                                             ALU.mult, ALU.mult))
        dve(lambda e: e.tensor_mul(A64, kg("k01re"), kg("k10re")))
        dve(lambda e: e.tensor_mul(B64, kg("k01im"), kg("k10im")))
        dve(lambda e: e.tensor_sub(A64, A64, B64))
        dve(lambda e: e.tensor_mul(B64, kg("k01re"), kg("k10im")))
        dve(lambda e: e.tensor_mul(E64, kg("k01im"), kg("k10re")))
        dve(lambda e: e.tensor_add(B64, B64, E64))
        dve(lambda e: e.tensor_mul(kg("Y_re"), A64, C64))
        dve(lambda e: e.tensor_mul(kg("Y_im"), B64, D64))
        dve(lambda e: e.tensor_sub(kg("Y_re"), kg("Y_re"), kg("Y_im")))
        dve(lambda e: e.tensor_mul(kg("Y_im"), A64, D64))
        dve(lambda e: e.tensor_mul(E64, B64, C64))
        dve(lambda e: e.tensor_add(kg("Y_im"), kg("Y_im"), E64))
        dve(lambda e: e.tensor_sub(kg("Y_re"), kg("k00re"), kg("Y_re")))
        dve(lambda e: e.tensor_sub(kg("Y_im"), kg("k00im"), kg("Y_im")))
        dve(lambda e: e.tensor_mul(kg("at_re"), kg("c_re"), kg("Y_re")))
        dve(lambda e: e.tensor_mul(E64, kg("c_im"), kg("Y_im")))
        dve(lambda e: e.tensor_sub(kg("at_re"), kg("at_re"), E64))
        dve(lambda e: e.tensor_mul(kg("at_im"), kg("c_re"), kg("Y_im")))
        dve(lambda e: e.tensor_mul(E64, kg("c_im"), kg("Y_re")))
        at_done = dve(lambda e: e.tensor_add(kg("at_im"), kg("at_im"), E64), inc=True)

        P.wait("tensor", s_dve, at_done)
        pe(lambda e: e.matmul(PS[1][0:64, 1024:1088], kg("at_re"), kg("E1_re"), start=True, stop=False))
        pe(lambda e: e.matmul(PS[1][0:64, 1024:1088], kg("at_im"), kg("E1_imn"), start=False, stop=True))
        pe(lambda e: e.matmul(PS[1][0:64, 1536:1600], kg("at_re"), kg("E1_im"), start=True, stop=False))
        pe_n = pe(lambda e: e.matmul(PS[1][0:64, 1536:1600], kg("at_im"), kg("E1_re"), start=False, stop=True), inc=True)
        P.wait("vector", s_pe, pe_n)
        dve(lambda e: e.tensor_copy(kg("Y_re"), PS[1][0:64, 1024:1088]))
        dve(lambda e: e.tensor_copy(kg("Y_im"), PS[1][0:64, 1536:1600]))
        dve(lambda e: e.tensor_mul(kg("Z_re"), kg("W_re"), kg("Y_re")))
        dve(lambda e: e.tensor_mul(E64, kg("W_im"), kg("Y_im")))
        dve(lambda e: e.tensor_sub(kg("Z_re"), kg("Z_re"), E64))
        dve(lambda e: e.tensor_mul(kg("Z_im"), kg("W_re"), kg("Y_im")))
        dve(lambda e: e.tensor_mul(E64, kg("W_im"), kg("Y_re")))
        z_done = dve(lambda e: e.tensor_add(kg("Z_im"), kg("Z_im"), E64), inc=True)
        P.wait("tensor", s_dve, z_done)
        pe(lambda e: e.matmul(PS[1][0:64, 1024:1088], kg("E2_re"), kg("Z_re"), start=True, stop=False))
        pe_n = pe(lambda e: e.matmul(PS[1][0:64, 1024:1088], kg("E2_imn"), kg("Z_im"), start=False, stop=True), inc=True)
        P.wait("scalar", s_pe, pe_n)
        kmat_done = act(lambda e: e.copy(kmat_bf[:], PS[1][0:64, 1024:1088]), inc=True)
        P.wait("sync", s_dve, zrow_done)
        dma("sync", lambda e: e.dma_start(out=bass.AP(kpad, 0, [[127, 1], [1, 127]]),
                                          in_=zrow[0:1, 0:127]), s_kp)
        P.wait("sync", s_act, kmat_done)
        dma("sync", lambda e: e.dma_start(out=bass.AP(kpad, 127, [[64, 64], [1, 64]]),
                                          in_=kmat_bf[:]), s_kp)
        P.wait("sync", s_kp, s_kp.n)
        hank_done = dma("sync", lambda e: e.dma_start(
            out=hank.ap()[:, 0:D].rearrange("q (d f) -> q d f", d=CH),
            in_=bass.AP(kpad, 0, [[1, 128], [128, CH], [1, 128]])), s_kp)
        P.wait("tensor", s_kp, hank_done)
        tt_ev = [0, 0]
        for i in range(8):
            sl = slice(i * 512, (i + 1) * 512)
            if tt_ev[i % 2]:
                P.wait("tensor", s_act, tt_ev[i % 2])
            pe_n = pe(lambda e, sl=sl, i=i: e.matmul(psv(1, i % 2), prev_sb[:], hank[:, sl],
                                                start=True, stop=True), inc=True)
            P.wait("scalar", s_pe, pe_n)
            tt_ev[i % 2] = act(lambda e, sl=sl, i=i: e.copy(ttoep[:, sl], psv(1, i % 2)), inc=True)
        ttoep_done = s_act.n
        bb_ev = [0, 0]
        for i in range(8):
            sl = slice(i * 512, (i + 1) * 512)
            if bb_ev[i % 2]:
                P.wait("tensor", s_dve, bb_ev[i % 2])
            pe_n = pe(lambda e, sl=sl, i=i: e.matmul(psv(0, i % 2), ones_sb[:],
                                                     bias_b[0:1, sl], start=True, stop=True), inc=True)
            P.wait("vector", s_pe, pe_n)
            bb_ev[i % 2] = dve(lambda e, sl=sl, i=i: e.tensor_copy(bias_b[:, sl], psv(0, i % 2)), inc=True)
        for bb in range(2):
            for g in range(4):
                set_bank(bb, g, "dve", s_dve.n)
        set_bank(1, 0, "act", ttoep_done)
        set_bank(1, 1, "act", ttoep_done)
        set_bank(1, 2, "act", kmat_done)
        set_bank(1, 3, "dve", s_dve.n)

        prepass_state = {}
        s_fst = [mksem("s_fst0"), mksem("s_fst1")]
        s_ppf = [mksem("s_ppf0"), mksem("s_ppf1")]

        def emit_prepass(upto=2 * CH):
            st = prepass_state
            if "sign_done" not in st:
                st["sign_done"] = {}
                st["ffn_win"] = [None, None]
                st["next_k"] = 0
                P.wait("gpsimd", s_ld, LD_FINAL)
            k0 = st["next_k"]
            k1 = min(upto, 2 * CH)
            st["next_k"] = max(k0, k1)
            stage_f32 = [qkv_sb[0][:].bitcast(F32)[:, 0:2048],
                         osc_bf[1][:].bitcast(F32)[:, 0:2048]]
            sign_done = st["sign_done"]
            for k in range(k0, max(k0, k1)):
                w, c = divmod(k, CH)
                if k >= 2:
                    enga, sga = sign_done[k - 2]
                    P.wait("gpsimd", s_act if enga == "act" else s_dve, sga)
                dma("gpsimd", lambda e, c=c, w=w, k=k: e.dma_start(
                    out=stage_f32[k % 2],
                    in_=wffnT.ap()[c * 128:(c + 1) * 128, w * 2048:(w + 1) * 2048]), s_ppf[k % 2])
                if k % 2 == 0:
                    P.wait("scalar", s_ppf[0], s_ppf[0].n)
                    if k >= 2:
                        P.wait("scalar", s_fst[0], (k // 2) * 16)
                    sg = act(lambda e: e.activation(
                        osc_bf[0][:, 0:2048], stage_f32[0], AF.Sign), inc=True)
                    sign_done[k] = ("act", sg)
                else:
                    P.wait("vector", s_ppf[1], s_ppf[1].n)
                    if k >= 2:
                        P.wait("vector", s_fst[1], (k // 2) * 16)
                    dve(lambda e: e.tensor_scalar(osc_bf[0][:, 2048:4096],
                                                  stage_f32[1], 0.0, None, ALU.is_gt))
                    sg = dve(lambda e: e.tensor_scalar(
                        osc_bf[0][:, 2048:4096], osc_bf[0][:, 2048:4096],
                        2.0, -1.0, ALU.mult, ALU.add), inc=True)
                    sign_done[k] = ("dve", sg)
                eng, sg = sign_done[k]
                P.wait("scalar", s_act if eng == "act" else s_dve, sg)
                dma("scalar", lambda e, c=c, w=w, k=k: e.dma_start(
                    out=ffn_bf.ap()[c * 128:(c + 1) * 128, w * 2048:(w + 1) * 2048],
                    in_=osc_bf[0][:, (k % 2) * 2048:(k % 2) * 2048 + 2048]), s_fst[k % 2])
                if c == CH - 1:
                    st["ffn_win"][w] = (s_fst[0].n, s_fst[1].n)
            if k1 < 2 * CH or "strips" in st:
                return
            st["strips"] = True
            for w in range(3):
                dma("gpsimd", lambda e, w=w: e.dma_start(
                    out=bass.AP(qkv_bf, w * 2048, [[QKVW, D], [1, 2048]]),
                    in_=bass.AP(wqkvT, w * 2048, [[QKVW, D], [1, 2048]])), s_ppq[w])
            for w in range(2):
                dma("gpsimd", lambda e, w=w: e.dma_start(
                    out=bass.AP(proj_bf, w * 2048, [[D, D], [1, 2048]]),
                    in_=bass.AP(wprojT, w * 2048, [[D, D], [1, 2048]])), s_ppp[w])

        wstate = {"n": 0, "loads": [0] * NWBUF}
        stream_gate = set()

        WQUEUE = ["sync", "sync", "sync"]

        def stream_sub(src_dram, row0, col0, ncols, gate_key):
            n = wstate["n"]
            bi = n % NWBUF
            qn = WQUEUE[bi]
            gk = (gate_key, col0 // COLW, qn)
            if gk not in stream_gate:
                w = col0 // COLW
                if gate_key == "ffn":
                    P.wait(qn, s_fst[0], prepass_state["ffn_win"][w][0])
                    P.wait(qn, s_fst[1], prepass_state["ffn_win"][w][1])
                elif gate_key == "qkv":
                    P.wait(qn, s_ppq[w], 16)
                else:
                    P.wait(qn, s_ppp[w], 16)
                stream_gate.add(gk)
            if n >= NWBUF:
                P.wait(qn, s_wfree, n - (NWBUF - 1))
            wstate["loads"][bi] += 1
            wait_val = wstate["loads"][bi] * 16
            Wfull = src_dram.shape[1]
            srcap = bass.AP(src_dram, row0 * Wfull + col0,
                            [[Wfull, 128], [128 * Wfull, CPS], [1, ncols]])
            dstap = wbuf[bi].ap()[:, 0:CPS * ncols].rearrange("p (u n) -> p u n", u=CPS)
            dma(qn, lambda e, s=srcap, d=dstap: e.dma_start(out=d, in_=s), s_w[bi])
            wstate["n"] += 1
            return bi, wait_val

        def emit_pass(src_dram, win, blocks_actT, gate_key):
            col0 = win * COLW
            for q in range(CH // CPS):
                bi, wv = stream_sub(src_dram, q * CPS * 128, col0, COLW, gate_key)
                P.wait("tensor", s_w[bi], wv)
                for cc in range(CPS):
                    c = q * CPS + cc
                    for bb in range(2):
                        for g in range(4):
                            if c == 0:
                                gate_bank(bb, g)
                            last = (cc == CPS - 1 and bb == 1 and g == 3)
                            fn = (lambda e, c=c, bb=bb, g=g, bi=bi, cc=cc:
                                  e.matmul(psv(bb, g),
                                           blocks_actT[bb][:, c * 128:(c + 1) * 128],
                                           wbuf[bi][:, cc * COLW + g * 512:cc * COLW + (g + 1) * 512],
                                           start=(c == 0), stop=(c == CH - 1)))
                            if last:
                                s_wfree.n += 1
                                P.emit("tensor", lambda e, f=fn: f(e).then_inc(s_wfree.h, 1))
                            else:
                                P.emit("tensor", fn)
            return s_wfree.n

        halves_ready = {}

        def stats_half(i, h, epi_gate):
            o = i * 12
            cols = slice(h * 2048, (h + 1) * 2048)
            P.wait("scalar", s_dve, epi_gate)
            act(lambda e, i=i, cols=cols, o=o, h=h: e.activation(
                ln_bf[:, 0:2048], res[i][:, cols], AF.Copy,
                accum_out=stat[:, o + 2 * h:o + 2 * h + 1]))
            v = act(lambda e, i=i, cols=cols, o=o, h=h: e.activation(
                ln_bf[:, 0:2048], res[i][:, cols], AF.Square,
                accum_out=stat[:, o + 2 * h + 1:o + 2 * h + 2]), inc=True)
            halves_ready.setdefault(i, {})[h] = v

        def ln_stats(i, with_rd):
            o = i * 12
            x = res[i]
            hv = halves_ready.pop(i, None)
            P.wait("scalar", s_dve, s_dve.n)
            P.wait("scalar", s_pe, lnbf_pe[0])
            P.wait("vector", s_pe, lnbf_pe[0])
            if hv is None:
                hvv = act(lambda e, o=o: e.activation(
                    ln_bf[:], x[:], AF.Square,
                    accum_out=stat[:, o + 1:o + 2]), inc=True)
                dve(lambda e, o=o: e.reduce_sum(stat[:, o:o + 1], x[:], axis=AX.X))
            else:
                hvv = max(hv.values())
                P.wait("vector", s_act, hvv)
                dve(lambda e: e.tensor_add(stat[:, o:o + 1], stat[:, o:o + 1], stat[:, o + 2:o + 3]))
                dve(lambda e: e.tensor_add(stat[:, o + 1:o + 2], stat[:, o + 1:o + 2], stat[:, o + 3:o + 4]))
            P.wait("vector", s_act, hvv)
            dve(lambda e: e.tensor_scalar(stat[:, o + 4:o + 5], stat[:, o:o + 1], 1.0 / D, None, ALU.mult))
            dve(lambda e: e.tensor_mul(stat[:, o + 9:o + 10], stat[:, o + 4:o + 5], stat[:, o + 4:o + 5]))
            v_ok = dve(lambda e: e.scalar_tensor_tensor(stat[:, o + 8:o + 9], stat[:, o + 1:o + 2],
                                                        1.0 / D, stat[:, o + 9:o + 10],
                                                        ALU.mult, ALU.subtract), inc=True)
            P.wait("scalar", s_dve, v_ok)
            act(lambda e: e.activation(stat[:, o + 9:o + 10], stat[:, o + 8:o + 9],
                                       AF.Ln, bias=epscol[:]))
            r_ok = act(lambda e: e.activation(stat[:, o + 5:o + 6], stat[:, o + 9:o + 10],
                                              AF.Exp, scale=-0.5), inc=True)
            P.wait("vector", s_act, r_ok)
            if with_rd:
                dve(lambda e: e.tensor_mul(stat[:, o + 9:o + 10], stat[:, o + 5:o + 6], dcol[:]))
                dve(lambda e: e.tensor_scalar(stat[:, o + 7:o + 8], stat[:, o + 9:o + 10],
                                              1.0, None, ALU.add))
                dve(lambda e: e.tensor_mul(stat[:, o + 6:o + 7], stat[:, o + 4:o + 5],
                                           stat[:, o + 9:o + 10]))
            outs = []
            for h in (0, 1):
                cols = slice(h * 2048, (h + 1) * 2048)
                outs.append(dve(lambda e, cols=cols, o=o: e.tensor_scalar(
                    ln_bf[:, cols], x[:, cols], stat[:, o + 4:o + 5],
                    stat[:, o + 5:o + 6], ALU.subtract, ALU.mult), inc=True))
            return tuple(outs)

        def transposes(i, src_fn, reads_lnbf=True, half_gates=None):
            evs = {}
            for c in range(CH):
                view = ps_bf(0, c % 4)
                if half_gates is not None and c % 16 == 0:
                    P.wait("tensor", s_dve, half_gates[c // 16])
                if c >= 4:
                    which, val = evs[c - 4]
                    P.wait("tensor", s_dve if which == 0 else s_act, val)
                pe_n = pe(lambda e, v=view, sap=src_fn(c): e.transpose(v, sap, ident_sb[:]),
                          inc=True)
                if c % 2 == 0:
                    P.wait("vector", s_pe, pe_n)
                    val = dve(lambda e, c=c, v=view: e.tensor_copy(
                        actT[i][:, c * 128:(c + 1) * 128], v), inc=True)
                    evs[c] = (0, val)
                else:
                    P.wait("scalar", s_pe, pe_n)
                    val = act(lambda e, c=c, v=view: e.copy(
                        actT[i][:, c * 128:(c + 1) * 128], v), inc=True)
                    evs[c] = (1, val)
            if reads_lnbf:
                lnbf_pe[0] = s_pe.n
            for g in range(4):
                wg, vg = evs[CH - 4 + g]
                set_bank(0, g, "dve" if wg == 0 else "act", vg)
            return s_dve.n, s_act.n

        def sub_conv(T, prestats=False):
            for i in (0, 1):
                hg = ln_stats(i, with_rd=True)
                gate_banks_for_pe()
                transposes(i, lambda c: ln_bf[:, c * 128:(c + 1) * 128], half_gates=hg)
            tp_d, tp_a = s_dve.n, s_act.n
            for qd in range(2):
                P.wait("tensor", s_dve, tp_d)
                P.wait("tensor", s_act, tp_a)
                gate_banks_for_pe()
                maxJ = 4 * (4 * qd + 3) + 3
                work = []
                for J in range(0, min(CH, maxJ + 1)):
                    for bb in range(2):
                        for g in range(4):
                            cgg = 4 * qd + g
                            if J > 4 * cgg + 3:
                                continue
                            if J <= 4 * cgg:
                                w, roff, poff = 512, 128 * (4 * cgg - J), 0
                            else:
                                w = 512 - 128 * (J - 4 * cgg)
                                roff, poff = 0, 128 * (J - 4 * cgg)
                            work.append((bb, g, J, w, roff, poff, 4 * cgg + 3))
                for idx, (bb, g, J, w, roff, poff, lastJ) in enumerate(work):
                    fn = (lambda e, bb=bb, g=g, J=J, w=w, roff=roff, poff=poff, lastJ=lastJ:
                          e.matmul(psv(bb, g, off=poff, w=w),
                                   actT[bb][:, J * 128:(J + 1) * 128],
                                   ttoep[:, roff:roff + w],
                                   start=(J == 0), stop=(J == lastJ)))
                    conv_pe = pe(fn, inc=(idx == len(work) - 1))
                P.wait("vector", s_pe, conv_pe)
                for bb in range(2):
                    o = bb * 12
                    cols = slice(qd * 2048, (qd + 1) * 2048)
                    dve(lambda e, bb=bb, cols=cols, o=o: e.scalar_tensor_tensor(
                        tr2[:], res[bb][:, cols], stat[:, o + 7:o + 8], PS[bb][:, 0:2048],
                        ALU.mult, ALU.add))
                    v = dve(lambda e, bb=bb, cols=cols, o=o: e.tensor_scalar(
                        res[bb][:, cols], tr2[:], stat[:, o + 6:o + 7], None,
                        ALU.subtract), inc=True)
                    for g in range(4):
                        set_bank(bb, g, "dve", v)
                    if prestats:
                        stats_half(bb, qd, s_dve.n)

        def stage1_ffn(i):
            hg = ln_stats(i, with_rd=False)
            gate_banks_for_pe()
            tp_d, tp_a = transposes(i, lambda c: ln_bf[:, c * 128:(c + 1) * 128],
                                    half_gates=hg)
            P.wait("tensor", s_dve, tp_d)
            P.wait("tensor", s_act, tp_a)
            gate_bank(1, 0)
            for c in range(CH):
                lg_pe = pe(lambda e, c=c: e.matmul(PS[1][:, 0:8],
                                           actT[i][:, c * 128:(c + 1) * 128],
                                           gate_sb[:, c * NEXP:(c + 1) * NEXP],
                                           start=(c == 0), stop=(c == CH - 1)),
                           inc=(c == CH - 1))
            P.wait("vector", s_pe, lg_pe)
            dve(lambda e: e.reduce_max(alph[:, 0:1], PS[1][:, 0:8], axis=AX.X))
            nm_ok = dve(lambda e: e.tensor_scalar(alph[:, 1:2], alph[:, 0:1], -1.0,
                                                  None, ALU.mult), inc=True)
            P.wait("scalar", s_dve, nm_ok)
            ex_ok = act(lambda e: e.activation(rout_f[:], PS[1][:, 0:8], AF.Exp,
                                               bias=alph[:, 1:2]), inc=True)
            set_bank(1, 0, "act", ex_ok)
            P.wait("vector", s_act, ex_ok)
            dve(lambda e: e.reduce_sum(alph[:, 0:1], rout_f[:], axis=AX.X))
            dve(lambda e: e.reciprocal(alph[:, 0:1], alph[:, 0:1]))
            gate_bank(1, 1)
            P.wait("vector", s_pe, last_rtp[0])
            rt_ok = dve(lambda e: e.tensor_scalar(rout_bf[:], rout_f[:], alph[:, 0:1],
                                                  None, ALU.mult), inc=True)
            P.wait("tensor", s_dve, rt_ok)
            last_rtp[0] = last_rtp[0]
            rtp = pe(lambda e: e.transpose(PS[1][:].bitcast(BF16)[0:8, 2048:2176],
                                           rout_bf[:], ident_sb[:]), inc=True)
            last_rtp[0] = rtp
            P.wait("scalar", s_pe, rtp)
            rT_ok = act(lambda e, i=i: e.copy(routT[i][:], PS[1][:].bitcast(BF16)[0:8, 2048:2176]), inc=True)
            set_bank(1, 1, "act", rT_ok)
            P.wait("tensor", s_act, rT_ok)
            INSLOT = [(0, 2), (1, 2), (0, 3), (1, 3)]
            ins_mults = [0, 0, 0, 0]
            for sl in INSLOT:
                gate_bank(*sl)
            for grp in range(8):
                pb, gsl = INSLOT[grp % 4]
                if ins_mults[grp % 4]:
                    P.wait("tensor", s_dve, ins_mults[grp % 4])
                for cc in range(4):
                    c = grp * 4 + cc
                    g_pe = pe(lambda e, c=c, cc=cc, i=i, pb=pb, gsl=gsl: e.matmul(
                        psv(pb, gsl, off=cc * 128, w=128),
                        ics_sb[:, c * 128:(c + 1) * 128], routT[i][:],
                        start=True, stop=True), inc=(cc == 3))
                P.wait("vector", s_pe, g_pe)
                ins_mults[grp % 4] = dve(lambda e, grp=grp, i=i, pb=pb, gsl=gsl: e.tensor_mul(
                    actT[i][:, grp * 512:(grp + 1) * 512],
                    actT[i][:, grp * 512:(grp + 1) * 512],
                    psv(pb, gsl)), inc=True)
                set_bank(pb, gsl, "dve", ins_mults[grp % 4])
            zT_d = s_dve.n
            osc_ev = [0, 0]
            gate_bank(0, 3)
            gate_bank(1, 3)
            if not osc_gate[0]:
                P.wait("scalar", s_fst[0], prepass_state["ffn_win"][1][0])
                P.wait("scalar", s_fst[1], prepass_state["ffn_win"][1][1])
                osc_gate[0] = True
            for gg in range(8):
                pb = gg % 2
                if osc_ev[pb]:
                    P.wait("tensor", s_act, osc_ev[pb])
                po = pe(lambda e, gg=gg, i=i, pb=pb: e.matmul(
                    psv(pb, 3), routT[i][:], ocs_sb[:, gg * 512:(gg + 1) * 512],
                    start=True, stop=True), inc=True)
                P.wait("scalar", s_pe, po)
                osc_ev[pb] = act(lambda e, gg=gg, i=i, pb=pb: e.copy(
                    osc_bf[i][:, gg * 512:(gg + 1) * 512], psv(pb, 3)), inc=True)
                set_bank(pb, 3, "act", osc_ev[pb])
            return zT_d, s_act.n

        def sub_ffn(T, prestats=False):
            epi_b = [0, 0]
            for i in (0, 1):
                stage1_ffn(i)
            for bb in range(2):
                dve(lambda e, bb=bb: e.tensor_add(res[bb][:], res[bb][:], bias_b[:]))
            for win in range(2):
                wf = emit_pass(ffn_bf, win, actT, "ffn")
                P.wait("vector", s_wfree, wf)
                for bb in range(2):
                    cols = slice(win * 2048, (win + 1) * 2048)
                    dve(lambda e, bb=bb, cols=cols: e.tensor_mul(
                        tr2[:], PS[bb][:, 0:2048], osc_bf[bb][:, cols]))
                    v = dve(lambda e, bb=bb, cols=cols: e.tensor_add(
                        res[bb][:, cols], res[bb][:, cols], tr2[:]), inc=True)
                    for g in range(4):
                        set_bank(bb, g, "dve", v)
                    if prestats:
                        stats_half(bb, win, s_dve.n)
                    epi_b[bb] = s_dve.n
            return epi_b

        def sub_attn(T, prestats=False):
            for i in (0, 1):
                hg = ln_stats(i, with_rd=False)
                gate_banks_for_pe()
                transposes(i, lambda c: ln_bf[:, c * 128:(c + 1) * 128], half_gates=hg)
            tp_d, tp_a = s_dve.n, s_act.n
            P.wait("tensor", s_dve, tp_d)
            P.wait("tensor", s_act, tp_a)
            kviewS = lambda b, k0, nk: bass.AP(qkv_sb[b].ap().tensor, k0 * 768 + 512,
                                               [[QKVW, 128], [768, nk], [1, 128]])
            vviewS = lambda b, k0, nk: bass.AP(qkv_sb[b].ap().tensor, k0 * 768 + 640,
                                               [[QKVW, 128], [768, nk], [1, 128]])
            qviewS = lambda b, g, k0, nk: bass.AP(qkv_sb[b].ap().tensor, k0 * 768 + g * 128,
                                                  [[QKVW, 128], [768, nk], [1, 128]])
            s_at = s_atn

            def poolop(fn):
                P.wait("gpsimd", s_at, s_at.n)
                s_at.n += 1
                P.emit("gpsimd", lambda e, fn=fn: fn(e).then_inc(s_at.h, 1))
                return s_at.n

            SUBSETS = {0: (0, 2), 1: (2, 3), 2: (5, 3)}
            out0_ok = 0
            for win in range(3):
                wf = emit_pass(qkv_bf, win, actT, "qkv")
                P.wait("scalar", s_wfree, wf)
                for bb in range(2):
                    cols = slice(win * 2048, (win + 1) * 2048)
                    v = act(lambda e, bb=bb, cols=cols: e.copy(
                        qkv_sb[bb][:, cols], PS[bb][:, 0:2048]), inc=True)
                    for g in range(4):
                        set_bank(bb, g, "act", v)
                qkv_ev = s_act.n
                k0, nk = SUBSETS[win]
                koff = k0 * 128
                P.wait("gpsimd", s_act, qkv_ev)
                poolop(lambda e, k0=k0, nk=nk, koff=koff: e.tensor_sub(
                    dkbuf[:, koff:koff + nk * 128], kviewS(1, k0, nk), kviewS(0, k0, nk)))
                poolop(lambda e, k0=k0, nk=nk, koff=koff: e.tensor_sub(
                    dvbuf[:, koff:koff + nk * 128], vviewS(1, k0, nk), vviewS(0, k0, nk)))
                for g in range(4):
                    pk = poolop(lambda e, g=g, k0=k0, nk=nk, koff=koff: e.tensor_mul(
                        prodb[:, 0:nk * 128], qviewS(1, g, k0, nk),
                        dkbuf[:, koff:koff + nk * 128]))
                    P.wait("vector", s_at, pk)
                    dve(lambda e, g=g, k0=k0, nk=nk: e.tensor_reduce(
                        bass.AP(alph.ap().tensor, 4 * k0 + g, [[64, 128], [4, nk]]),
                        prodb.ap()[:, 0:nk * 128].rearrange("p (k d) -> p k d", k=nk),
                        axis=AX.X, op=ALU.add), inc=True)
                    if g < 3:
                        P.wait("gpsimd", s_dve, s_dve.n)
                sc_ok = s_dve.n
                P.wait("scalar", s_dve, sc_ok)
                acols = slice(32 + 4 * k0, 32 + 4 * (k0 + nk))
                scols = slice(4 * k0, 4 * (k0 + nk))
                al_ok = act(lambda e, acols=acols, scols=scols: e.activation(
                    alph[:, acols], alph[:, scols], AF.Exp, scale=-ATT_SCALE), inc=True)
                P.wait("vector", s_act, al_ok)
                dve(lambda e, acols=acols: e.tensor_scalar(alph[:, acols], alph[:, acols],
                                                           1.0, None, ALU.add))
                sig_ok = dve(lambda e, acols=acols: e.reciprocal(alph[:, acols], alph[:, acols]),
                             inc=True)
                for g in range(4):
                    out0_ok = act(lambda e, g=g, k0=k0, nk=nk: e.copy(
                        qviewS(0, g, k0, nk), vviewS(0, k0, nk)), inc=True)
                P.wait("gpsimd", s_dve, sig_ok)
                for g in range(4):
                    avw = bass.AP(alph.ap().tensor, 32 + 4 * k0 + g, [[64, 128], [4, nk], [0, 128]])
                    poolop(lambda e, g=g, avw=avw, k0=k0, nk=nk, koff=koff: e.tensor_mul(
                        prodb[:, 0:nk * 128], dvbuf[:, koff:koff + nk * 128], avw))
                    poolop(lambda e, g=g, k0=k0, nk=nk: e.tensor_add(
                        qviewS(1, g, k0, nk), prodb[:, 0:nk * 128], vviewS(0, k0, nk)))
            at_ok = s_at.n
            P.wait("tensor", s_at, at_ok)
            P.wait("tensor", s_act, out0_ok)
            for i in (0, 1):
                P.wait("tensor", s_atn, at_ok)
                gate_banks_for_pe()
                transposes(i, lambda c, i=i: qkv_sb[i][:, (c // 4) * 768 + (c % 4) * 128:(c // 4) * 768 + (c % 4) * 128 + 128], reads_lnbf=False)
            tp2_d, tp2_a = s_dve.n, s_act.n
            P.wait("tensor", s_dve, tp2_d)
            P.wait("tensor", s_act, tp2_a)
            for win in range(2):
                wf = emit_pass(proj_bf, win, actT, "proj")
                P.wait("vector", s_wfree, wf)
                for bb in range(2):
                    cols = slice(win * 2048, (win + 1) * 2048)
                    v = dve(lambda e, bb=bb, cols=cols: e.tensor_add(
                        res[bb][:, cols], res[bb][:, cols], PS[bb][:, 0:2048]), inc=True)
                    for g in range(4):
                        set_bank(bb, g, "dve", v)
                    if prestats:
                        stats_half(bb, win, s_dve.n)

        for T in range(NT):
            if T > 0:
                for i in (0, 1):
                    s, r0 = (0, T * 128) if i == 0 else (1, T * 128)
                    P.wait("gpsimd", s_st[i], 16 * T)
                    dma("gpsimd", lambda e, i=i, s=s, r0=r0: e.dma_start(
                        out=res[i][:], in_=x_in.ap()[s, r0:r0 + 128, :]), s_x)
            x_ready = s_x.n
            for eng in ("vector", "scalar"):
                P.wait(eng, s_x, x_ready)
            if SUBLAYERS[0]:
                sub_conv(T, prestats=SUBLAYERS[1])
            if T == 0:
                emit_prepass()
            if SUBLAYERS[1]:
                sub_ffn(T, prestats=SUBLAYERS[2])
            if SUBLAYERS[2]:
                sub_attn(T, prestats=SUBLAYERS[3])
            if SUBLAYERS[3]:
                epi_b = sub_ffn(T)
            else:
                epi_b = [s_dve.n, s_dve.n]
            for i in (0, 1):
                s, r0 = (0, T * 128) if i == 0 else (1, T * 128)
                P.wait("gpsimd", s_dve, epi_b[i])
                dma("gpsimd", lambda e, i=i, s=s, r0=r0: e.dma_start(
                    out=x_out.ap()[s, r0:r0 + 128, :], in_=res[i][:]), s_st[i])

        P.wait("sync", s_st[0], s_st[0].n)
        P.wait("sync", s_st[1], s_st[1].n)

        with nc.Block() as block:
            P.replay(block)

    return nc


def kernel(**inputs):
    consts = _host_consts()
    u = np.ascontiguousarray(inputs["L0_attn_a_u"])
    lam = inputs["L0_attn_a_Lambda_ri"]
    p_ri = inputs["L0_attn_a_p_ri"]
    q_ri = inputs["L0_attn_a_q_ri"]
    vc = inputs["L0_attn_a_Vc_ri"]
    ct = inputs["L0_attn_a_Ct_ri"]
    f32c = lambda a: np.ascontiguousarray(np.asarray(a), dtype=np.float32)
    base = {
        "wqkvT": f32c(np.asarray(inputs["L0_attn_b_weight_qkv"]).T),
        "wprojT": f32c(np.asarray(inputs["L0_attn_b_weight_proj"]).T),
        "wffnT": f32c(np.asarray(inputs["L0_ffn_weight"]).T),
        "gateT": f32c(np.asarray(inputs["L0_ffn_gate_w"]).T),
        "ics": f32c(inputs["L0_ffn_in_channel_scale"]),
        "ocs": f32c(inputs["L0_ffn_out_channel_scale"]),
        "bias_in": f32c(np.asarray(inputs["L0_ffn_bias"]).reshape(1, -1)),
        "lam_re": f32c(np.asarray(lam)[:, 0].reshape(64, 1)),
        "lam_im": f32c(np.asarray(lam)[:, 1].reshape(64, 1)),
        "p_re": f32c(np.asarray(p_ri)[:, 0].reshape(64, 1)),
        "p_im": f32c(np.asarray(p_ri)[:, 1].reshape(64, 1)),
        "q_re": f32c(np.asarray(q_ri)[:, 0].reshape(64, 1)),
        "q_im": f32c(np.asarray(q_ri)[:, 1].reshape(64, 1)),
        "ct_re": f32c(np.asarray(ct)[0, :, 0].reshape(64, 1)),
        "ct_im": f32c(np.asarray(ct)[0, :, 1].reshape(64, 1)),
        "vcT_re": f32c(np.asarray(vc)[:, :, 0].T),
        "vcT_im": f32c(np.asarray(vc)[:, :, 1].T),
        "bp_in": f32c(np.asarray(inputs["L0_attn_a_Bp"]).reshape(64, 1)),
        "d_in": f32c(np.asarray(inputs["L0_attn_a_D"]).reshape(1, 1)),
        "ls_in": f32c(np.asarray(inputs["L0_attn_a_log_step"]).reshape(1, 1)),
    }
    base.update(consts)

    nc = build_program()
    in_maps = []
    for c in range(NCORE):
        m = dict(base)
        m["x_in"] = np.ascontiguousarray(u[:, c * 1024:(c + 1) * 1024, :])
        in_maps.append(m)
    res = run_bass_kernel_spmd(nc, in_maps, core_ids=list(range(NCORE)))
    out = np.concatenate([r["x_out"] for r in res.results], axis=1)
    return np.ascontiguousarray(out, dtype=np.float32)


# version 20
# speedup vs baseline: 1.0962x; 1.0298x over previous
import numpy as np
import ml_dtypes
import concourse.bass as bass
import concourse.mybir as mybir
from concourse.bass_utils import run_bass_kernel_spmd
from contextlib import ExitStack

F32 = mybir.dt.float32
BF16 = mybir.dt.bfloat16
AF = mybir.ActivationFunctionType
ALU = mybir.AluOpType
AX = mybir.AxisListType

D = 4096
PB = 128
CH = D // PB
NCORE = 8
NT = 8
QKVW = 6144
NEXP = 8
L = 4096
EPS = 1e-5
ATT_SCALE = float(1.0 / np.sqrt(128.0))
CPS = 4
COLW = 2048
NWBUF = 3

SUBLAYERS = (True, True, True, True)

ENGINES = ("sync", "gpsimd", "scalar", "vector", "tensor")


class Sem:
    def __init__(self, handle, name):
        self.h = handle
        self.name = name
        self.n = 0


class Prog:
    def __init__(self):
        self.ops = {e: [] for e in ENGINES}
        self.waited = {e: {} for e in ENGINES}

    def emit(self, eng, fn):
        self.ops[eng].append(fn)

    def wait(self, eng, sem, val):
        if val > 0 and self.waited[eng].get(sem.name, -1) < val:
            self.waited[eng][sem.name] = val
            self.ops[eng].append(lambda e, s=sem.h, v=val: e.wait_ge(s, v))

    def replay(self, block):
        for eng in ENGINES:
            ops = self.ops[eng]
            getattr(block, eng)(lambda e, ops=ops: [op(e) for op in ops])


def _host_consts():
    j = np.arange(L)
    t32 = (j / L).astype(np.float32)
    Om = np.exp(np.complex64(-2j * np.pi) * t32).astype(np.complex64)
    om = ((1 - Om) / (1 + Om)).astype(np.complex64)
    cc = (np.complex64(2.0) / (1 + Om)).astype(np.complex64)

    i64 = np.arange(64)
    E1 = np.exp(2j * np.pi * np.outer(i64, i64) / 64.0)
    Wt = np.exp(2j * np.pi * np.outer(i64, i64) / 4096.0)
    E2s = E1 / 4096.0

    f32 = lambda a: np.ascontiguousarray(a, dtype=np.float32)
    return {
        "om_re": f32(np.broadcast_to(om.real, (64, L))),
        "om_im": f32(np.broadcast_to(om.imag, (64, L))),
        "c_re": f32(cc.real.reshape(64, 64)),
        "c_im": f32(cc.imag.reshape(64, 64)),
        "E1_re": f32(E1.real), "E1_im": f32(E1.imag), "E1_imn": f32(-E1.imag),
        "W_re": f32(Wt.real), "W_im": f32(Wt.imag),
        "E2_re": f32(E2s.real), "E2_imn": f32(-E2s.imag),
        "ident_bf": np.eye(128, dtype=ml_dtypes.bfloat16),
        "prev_bf": np.eye(128, dtype=ml_dtypes.bfloat16)[::-1].copy(),
        "ones_f": np.ones((1, 128), dtype=np.float32),
    }


def build_program():
    nc = bass.Bass("TRN2", target_bir_lowering=False)

    x_in = nc.dram_tensor("x_in", [2, 1024, D], F32, kind="ExternalInput")
    x_out = nc.dram_tensor("x_out", [2, 1024, D], F32, kind="ExternalOutput")
    wqkvT = nc.dram_tensor("wqkvT", [D, QKVW], F32, kind="ExternalInput")
    wprojT = nc.dram_tensor("wprojT", [D, D], F32, kind="ExternalInput")
    wffnT = nc.dram_tensor("wffnT", [D, D], F32, kind="ExternalInput")
    gateT = nc.dram_tensor("gateT", [D, NEXP], F32, kind="ExternalInput")
    ics = nc.dram_tensor("ics", [NEXP, D], F32, kind="ExternalInput")
    ocs = nc.dram_tensor("ocs", [NEXP, D], F32, kind="ExternalInput")
    bias_in = nc.dram_tensor("bias_in", [1, D], F32, kind="ExternalInput")
    s4_small = {}
    for nm in ["lam_re", "lam_im", "p_re", "p_im", "q_re", "q_im",
               "ct_re", "ct_im", "bp_in"]:
        s4_small[nm] = nc.dram_tensor(nm, [64, 1], F32, kind="ExternalInput")
    vcT_re = nc.dram_tensor("vcT_re", [64, 64], F32, kind="ExternalInput")
    vcT_im = nc.dram_tensor("vcT_im", [64, 64], F32, kind="ExternalInput")
    d_in = nc.dram_tensor("d_in", [1, 1], F32, kind="ExternalInput")
    ls_in = nc.dram_tensor("ls_in", [1, 1], F32, kind="ExternalInput")
    cn = {}
    for name, shp, dt in [
        ("om_re", [64, L], F32), ("om_im", [64, L], F32),
        ("c_re", [64, 64], F32), ("c_im", [64, 64], F32),
        ("E1_re", [64, 64], F32), ("E1_im", [64, 64], F32), ("E1_imn", [64, 64], F32),
        ("W_re", [64, 64], F32), ("W_im", [64, 64], F32),
        ("E2_re", [64, 64], F32), ("E2_imn", [64, 64], F32),
        ("ident_bf", [128, 128], BF16), ("prev_bf", [128, 128], BF16),
        ("ones_f", [1, 128], F32),
    ]:
        cn[name] = nc.dram_tensor(name, shp, dt, kind="ExternalInput")
    qkv_bf = nc.dram_tensor("qkv_bf", [D, QKVW], BF16, kind="Internal")
    proj_bf = nc.dram_tensor("proj_bf", [D, D], BF16, kind="Internal")
    ffn_bf = nc.dram_tensor("ffn_bf", [D, D], BF16, kind="Internal")
    kk_scr = nc.dram_tensor("kk_scr", [8, L], F32, kind="Internal")
    kpad = nc.dram_tensor("kpad", [127 + L], BF16, kind="Internal")

    P = Prog()

    with ExitStack() as stack:
        def sb(name, shape, dt):
            return stack.enter_context(nc.sbuf_tensor(name, shape, dt))

        def mksem(name):
            return Sem(stack.enter_context(nc.semaphore(name)), name)

        res = [sb(f"res{b}", [PB, D], F32) for b in range(2)]
        actT = [sb(f"actT{b}", [PB, D], BF16) for b in range(2)]
        wbuf = [sb(f"wbuf{i}", [PB, CPS * COLW], BF16) for i in range(NWBUF)]
        bias_b = sb("bias_b", [PB, D], F32)
        ttoep = sb("ttoep", [PB, D], BF16)
        qkv_sb = [sb(f"qkvsb{b}", [PB, QKVW], BF16) for b in range(2)]
        osc_bf = [sb(f"oscbf{b}", [PB, D], BF16) for b in range(2)]
        ln_bf = sb("ln_bf", [PB, D], BF16)
        tr2 = sb("tr2", [PB, D // 2], F32)
        prodb = sb("prodb", [PB, 1024], F32)
        gate_sb = sb("gate_sb", [PB, CH * NEXP], BF16)
        ics_sb = sb("ics_sb", [NEXP, D], BF16)
        ocs_sb = sb("ocs_sb", [NEXP, D], BF16)
        ident_sb = sb("ident_sb", [128, 128], BF16)
        prev_sb = sb("prev_sb", [128, 128], BF16)
        ones_sb = sb("ones_sb", [1, 128], F32)
        routT = [sb(f"routT{b}", [NEXP, PB], BF16) for b in range(2)]
        rout_f = sb("rout_f", [PB, NEXP], F32)
        rout_bf = sb("rout_bf", [PB, NEXP], BF16)
        stat = sb("stat", [PB, 24], F32)
        epscol = sb("epscol", [PB, 1], F32)
        dcol = sb("dcol", [PB, 1], F32)
        alph = sb("alph", [PB, 64], F32)
        kg_a2 = sb("kg_a2", [64, 1], F32)
        kg_dre = tr2[0:64, 1024:2048]
        kg_dim = prodb[0:64, 0:1024]
        kg_t = qkv_sb[1][:].bitcast(F32)[0:64, 2048:3072]
        kk_small = tr2[0:4, 0:1024]
        kgv = sb("kgv", [64, 24], F32)
        HL = L // 4
        kg64 = sb("kg64", [64, 64 * 26], F32)
        kmat_bf = sb("kmat_bf", [64, 64], BF16)
        zrow = sb("zrow", [1, 128], BF16)
        s4c_sb = sb("s4c_sb", [64, 12], F32)
        misc1 = sb("misc1", [1, 8], F32)
        hank = qkv_sb[1]
        dkbuf = tr2[:, 0:1024]
        dvbuf = tr2[:, 1024:2048]

        PS = [stack.enter_context(nc.psum_tensor(f"PS{bb}", [PB, 2048], F32))
              for bb in range(2)]

        def psv(bb, g, off=0, w=512):
            return PS[bb][:, g * 512 + off:g * 512 + off + w]

        def ps_bf(bb, g, cols=128, off=0):
            return PS[bb][:].bitcast(BF16)[:, g * 1024 + off:g * 1024 + off + cols]

        s_x = mksem("s_x")
        s_w = [mksem(f"s_w{i}") for i in range(NWBUF)]
        s_wfree = mksem("s_wfree")
        s_pe = mksem("s_pe")
        s_dve = mksem("s_dve")
        s_act = mksem("s_act")
        s_pool = mksem("s_pool")
        s_ppq = [mksem(f"s_ppq{i}") for i in range(3)]
        s_ppp = [mksem(f"s_ppp{i}") for i in range(2)]
        s_ldp = mksem("s_ldp")
        s_st = [mksem("s_st0"), mksem("s_st1")]
        s_ld = mksem("s_ld")
        s_kb = mksem("s_kb")
        s_kp = mksem("s_kp")
        s_atn = mksem("s_atn")

        def pe(fn, inc=False):
            if inc:
                s_pe.n += 1
                P.emit("tensor", lambda e, fn=fn: fn(e).then_inc(s_pe.h, 1))
            else:
                P.emit("tensor", fn)
            return s_pe.n

        def dve(fn, inc=True):
            P.wait("vector", s_dve, s_dve.n)
            s_dve.n += 1
            P.emit("vector", lambda e, fn=fn: fn(e).then_inc(s_dve.h, 1))
            return s_dve.n

        def act(fn, inc=True):
            P.wait("scalar", s_act, s_act.n)
            s_act.n += 1
            P.emit("scalar", lambda e, fn=fn: fn(e).then_inc(s_act.h, 1))
            return s_act.n

        def dma(eng, fn, sem):
            sem.n += 16
            P.emit(eng, lambda e, fn=fn: fn(e).then_inc(sem.h, 16))
            return sem.n

        lnbf_pe = [0]
        last_rtp = [0]
        osc_gate = [False]

        bank_free = {(bb, g): ("dve", 0) for bb in range(2) for g in range(4)}

        def set_bank(bb, g, eng, val):
            bank_free[(bb, g)] = (eng, val)

        def gate_bank(bb, g):
            eng, val = bank_free[(bb, g)]
            P.wait("tensor", s_dve if eng == "dve" else s_act, val)

        def gate_banks_for_pe():
            for bb in range(2):
                for g in range(4):
                    gate_bank(bb, g)

        dma("sync", lambda e: e.dma_start(out=ident_sb[:], in_=cn["ident_bf"].ap()), s_ld)
        dma("sync", lambda e: e.dma_start(out=prev_sb[:], in_=cn["prev_bf"].ap()), s_ld)
        dma("sync", lambda e: e.dma_start(out=ones_sb[:], in_=cn["ones_f"].ap()), s_ld)
        dma("sync", lambda e: e.dma_start(out=bias_b[0:1, :], in_=bias_in.ap()), s_ld)

        KG = {}
        _slot = [0]

        def kg_new(name):
            KG[name] = _slot[0]
            _slot[0] += 1
            return kg64[:, KG[name] * 64:(KG[name] + 1) * 64]

        def kg(name):
            return kg64[:, KG[name] * 64:(KG[name] + 1) * 64]

        for name in ["c_re", "c_im", "E1_re", "E1_im", "E1_imn", "W_re", "W_im",
                     "E2_re", "E2_imn"]:
            dst = kg_new(name)
            dma("sync", lambda e, d=dst, s=cn[name]: e.dma_start(out=d, in_=s.ap()), s_ld)
        dst = kg_new("vc_re")
        dma("sync", lambda e, d=dst: e.dma_start(out=d, in_=vcT_re.ap()), s_ld)
        dst = kg_new("vc_im")
        dma("sync", lambda e, d=dst: e.dma_start(out=d, in_=vcT_im.ap()), s_ld)
        for nm in ["at_re", "at_im", "Y_re", "Y_im", "Z_re", "Z_im",
                   "k00re", "k01re", "k10re", "k11re", "k00im", "k01im",
                   "k10im", "k11im"]:
            kg_new(nm)

        S4C = {}
        for i, nm in enumerate(["lam_re", "lam_im", "p_re", "p_im", "q_re",
                                "q_im", "ct_re", "ct_im", "bp_in"]):
            S4C[nm] = i
            dma("sync", lambda e, i=i, t=s4_small[nm]: e.dma_start(
                out=s4c_sb[:, i:i + 1], in_=t.ap()), s_ld)

        def s4c(nm):
            return s4c_sb[:, S4C[nm]:S4C[nm] + 1]

        dma("sync", lambda e: e.dma_start(out=misc1[0:1, 0:1], in_=d_in.ap()), s_ld)
        dma("sync", lambda e: e.dma_start(out=misc1[0:1, 1:2], in_=ls_in.ap()), s_ld)
        dma("gpsimd", lambda e: e.dma_start(
            out=gate_sb.ap().rearrange("p (c x) -> p c x", c=CH),
            in_=bass.AP(gateT, 0, [[NEXP, 128], [128 * NEXP, CH], [1, NEXP]])), s_ldp)
        dma("gpsimd", lambda e: e.dma_start(out=ics_sb[:], in_=ics.ap()), s_ldp)
        dma("gpsimd", lambda e: e.dma_start(out=ocs_sb[:], in_=ocs.ap()), s_ldp)
        LD_FINAL = s_ld.n
        LD_POOL = s_ldp.n
        for eng in ("vector", "scalar", "tensor"):
            P.wait(eng, s_ld, LD_FINAL)
            P.wait(eng, s_ldp, LD_POOL)

        dma("gpsimd", lambda e: e.dma_start(out=res[0][:], in_=x_in.ap()[0, 0:128, :]), s_x)
        dma("gpsimd", lambda e: e.dma_start(out=res[1][:], in_=x_in.ap()[1, 0:128, :]), s_x)

        dve(lambda e: e.memset(epscol[:], EPS))
        zrow_done = dve(lambda e: e.memset(zrow[:], 0.0), inc=True)

        pe_n = pe(lambda e: e.matmul(PS[0][:, 0:1], ones_sb[:], misc1[0:1, 0:1],
                                     start=True, stop=True), inc=True)
        P.wait("vector", s_pe, pe_n)
        dve(lambda e: e.tensor_copy(dcol[:], PS[0][:, 0:1]))

        act(lambda e: e.activation(misc1[0:1, 2:3], misc1[0:1, 1:2], AF.Exp, scale=-1.0))
        a2_done = act(lambda e: e.mul(misc1[0:1, 3:4], misc1[0:1, 2:3], 2.0), inc=True)
        P.wait("tensor", s_act, a2_done)
        pe_n = pe(lambda e: e.matmul(PS[0][0:64, 512:513], ones_sb[:, 0:64],
                                     misc1[0:1, 3:4], start=True, stop=True), inc=True)
        P.wait("vector", s_pe, pe_n)
        a2col_done = dve(lambda e: e.tensor_copy(kg_a2[:], PS[0][0:64, 512:513]), inc=True)

        pe(lambda e: e.matmul(PS[0][0:64, 1024:1025], kg("vc_re"), s4c("bp_in"),
                              start=True, stop=True))
        pe_n = pe(lambda e: e.matmul(PS[0][0:64, 1536:1537], kg("vc_im"), s4c("bp_in"),
                                     start=True, stop=True), inc=True)
        P.wait("vector", s_pe, pe_n)
        dve(lambda e: e.tensor_copy(kgv[:, 0:1], PS[0][0:64, 1024:1025]))
        dve(lambda e: e.tensor_copy(kgv[:, 1:2], PS[0][0:64, 1536:1537]))

        def cmul_conj(dr, di, are, aim, bre, bim):
            dve(lambda e: e.tensor_mul(kgv[:, 20:21], are, bre))
            dve(lambda e: e.tensor_mul(dr, aim, bim))
            dve(lambda e: e.tensor_add(dr, dr, kgv[:, 20:21]))
            dve(lambda e: e.tensor_mul(kgv[:, 20:21], are, bim))
            dve(lambda e: e.tensor_mul(di, aim, bre))
            dve(lambda e: e.tensor_sub(di, kgv[:, 20:21], di))

        cmul_conj(kgv[:, 2:3], kgv[:, 6:7], s4c("ct_re"), s4c("ct_im"), kgv[:, 0:1], kgv[:, 1:2])
        cmul_conj(kgv[:, 3:4], kgv[:, 7:8], s4c("ct_re"), s4c("ct_im"), s4c("p_re"), s4c("p_im"))
        cmul_conj(kgv[:, 4:5], kgv[:, 8:9], s4c("q_re"), s4c("q_im"), kgv[:, 0:1], kgv[:, 1:2])
        cmul_conj(kgv[:, 5:6], kgv[:, 9:10], s4c("q_re"), s4c("q_im"), s4c("p_re"), s4c("p_im"))
        vv_ok = dve(lambda e: e.tensor_scalar(kgv[:, 10:14], kgv[:, 6:10], -1.0, None, ALU.mult),
                    inc=True)
        t2v = tr2[0:64, 0:HL]
        for hf in range(4):
            if hf == 0:
                P.wait("sync", s_dve, vv_ok)
            else:
                P.wait("sync", s_dve, s_dve.n)
            ld0 = dma("sync", lambda e, hf=hf: e.dma_start(
                out=kg_dre[:], in_=cn["om_re"].ap()[:, hf * HL:(hf + 1) * HL]), s_kb)
            ld1 = dma("sync", lambda e, hf=hf: e.dma_start(
                out=kg_dim[:], in_=cn["om_im"].ap()[:, hf * HL:(hf + 1) * HL]), s_kb)
            P.wait("vector", s_kb, ld1)
            dve(lambda e: e.tensor_scalar(kg_dre[:], kg_dre[:], kg_a2[:], s4c("lam_re"),
                                          ALU.mult, ALU.subtract))
            dn_dve = dve(lambda e: e.tensor_scalar(kg_dim[:], kg_dim[:], kg_a2[:], s4c("lam_im"),
                                                   ALU.mult, ALU.subtract), inc=True)
            P.wait("scalar", s_dve, dn_dve)
            sq_done = act(lambda e: e.activation(kg_t[:], kg_dre[:], AF.Square), inc=True)
            P.wait("vector", s_act, sq_done)
            dve(lambda e: e.tensor_mul(t2v, kg_dim[:], kg_dim[:]))
            dve(lambda e: e.tensor_add(kg_t[:], kg_t[:], t2v))
            dve(lambda e: e.reciprocal(kg_t[:], kg_t[:]))
            dve(lambda e: e.tensor_mul(kg_dre[:], kg_dre[:], kg_t[:]))
            inv_done = dve(lambda e: e.scalar_tensor_tensor(kg_dim[:], kg_dim[:], -1.0, kg_t[:],
                                                            ALU.mult, ALU.mult), inc=True)
            P.wait("tensor", s_dve, inv_done)
            for jc in range(2):
                sl = slice(jc * 512, (jc + 1) * 512)
                gcol = hf * HL + jc * 512
                pe(lambda e, sl=sl: e.matmul(PS[1][0:4, 0:512], kgv[:, 2:6], kg_dre[:, sl],
                                             start=True, stop=False))
                pe(lambda e, sl=sl: e.matmul(PS[1][0:4, 0:512], kgv[:, 10:14], kg_dim[:, sl],
                                             start=False, stop=True))
                pe(lambda e, sl=sl: e.matmul(PS[1][0:4, 512:1024], kgv[:, 6:10], kg_dre[:, sl],
                                             start=True, stop=False))
                pe_n = pe(lambda e, sl=sl: e.matmul(PS[1][0:4, 512:1024], kgv[:, 2:6], kg_dim[:, sl],
                                                    start=False, stop=True), inc=True)
                P.wait("vector", s_pe, pe_n)
                P.wait("vector", s_kb, s_kb.n)
                dve(lambda e: e.tensor_copy(kk_small[:, 0:512], PS[1][0:4, 0:512]))
                ev_ok = dve(lambda e: e.tensor_copy(kk_small[:, 512:1024], PS[1][0:4, 512:1024]),
                            inc=True)
                P.wait("sync", s_dve, ev_ok)
                dma("sync", lambda e, gcol=gcol: e.dma_start(
                    out=bass.AP(kk_scr, gcol, [[L, 4], [1, 512]]),
                    in_=kk_small[:, 0:512]), s_kb)
                kkd = dma("sync", lambda e, gcol=gcol: e.dma_start(
                    out=bass.AP(kk_scr, 4 * L + gcol, [[L, 4], [1, 512]]),
                    in_=kk_small[:, 512:1024]), s_kb)
                P.wait("tensor", s_dve, ev_ok)
        P.wait("sync", s_kb, s_kb.n)
        for i, nm in enumerate(["k00re", "k01re", "k10re", "k11re",
                                "k00im", "k01im", "k10im", "k11im"]):
            dma("sync", lambda e, d=kg(nm), i=i: e.dma_start(
                out=d, in_=bass.AP(kk_scr, i * L, [[64, 64], [1, 64]])), s_kb)
        kvec_loaded = s_kb.n

        P.wait("vector", s_kb, kvec_loaded)
        A64 = tr2[0:64, 0:64]
        B64 = tr2[0:64, 64:128]
        C64 = tr2[0:64, 128:192]
        D64 = tr2[0:64, 192:256]
        E64 = tr2[0:64, 256:320]
        dve(lambda e: e.tensor_scalar(A64, kg("k11re"), 1.0, None, ALU.add))
        dve(lambda e: e.tensor_mul(B64, A64, A64))
        dve(lambda e: e.tensor_mul(C64, kg("k11im"), kg("k11im")))
        dve(lambda e: e.tensor_add(B64, B64, C64))
        dve(lambda e: e.reciprocal(B64, B64))
        dve(lambda e: e.tensor_mul(C64, A64, B64))
        dve(lambda e: e.scalar_tensor_tensor(D64, kg("k11im"), -1.0, B64,
                                             ALU.mult, ALU.mult))
        dve(lambda e: e.tensor_mul(A64, kg("k01re"), kg("k10re")))
        dve(lambda e: e.tensor_mul(B64, kg("k01im"), kg("k10im")))
        dve(lambda e: e.tensor_sub(A64, A64, B64))
        dve(lambda e: e.tensor_mul(B64, kg("k01re"), kg("k10im")))
        dve(lambda e: e.tensor_mul(E64, kg("k01im"), kg("k10re")))
        dve(lambda e: e.tensor_add(B64, B64, E64))
        dve(lambda e: e.tensor_mul(kg("Y_re"), A64, C64))
        dve(lambda e: e.tensor_mul(kg("Y_im"), B64, D64))
        dve(lambda e: e.tensor_sub(kg("Y_re"), kg("Y_re"), kg("Y_im")))
        dve(lambda e: e.tensor_mul(kg("Y_im"), A64, D64))
        dve(lambda e: e.tensor_mul(E64, B64, C64))
        dve(lambda e: e.tensor_add(kg("Y_im"), kg("Y_im"), E64))
        dve(lambda e: e.tensor_sub(kg("Y_re"), kg("k00re"), kg("Y_re")))
        dve(lambda e: e.tensor_sub(kg("Y_im"), kg("k00im"), kg("Y_im")))
        dve(lambda e: e.tensor_mul(kg("at_re"), kg("c_re"), kg("Y_re")))
        dve(lambda e: e.tensor_mul(E64, kg("c_im"), kg("Y_im")))
        dve(lambda e: e.tensor_sub(kg("at_re"), kg("at_re"), E64))
        dve(lambda e: e.tensor_mul(kg("at_im"), kg("c_re"), kg("Y_im")))
        dve(lambda e: e.tensor_mul(E64, kg("c_im"), kg("Y_re")))
        at_done = dve(lambda e: e.tensor_add(kg("at_im"), kg("at_im"), E64), inc=True)

        P.wait("tensor", s_dve, at_done)
        pe(lambda e: e.matmul(PS[1][0:64, 1024:1088], kg("at_re"), kg("E1_re"), start=True, stop=False))
        pe(lambda e: e.matmul(PS[1][0:64, 1024:1088], kg("at_im"), kg("E1_imn"), start=False, stop=True))
        pe(lambda e: e.matmul(PS[1][0:64, 1536:1600], kg("at_re"), kg("E1_im"), start=True, stop=False))
        pe_n = pe(lambda e: e.matmul(PS[1][0:64, 1536:1600], kg("at_im"), kg("E1_re"), start=False, stop=True), inc=True)
        P.wait("vector", s_pe, pe_n)
        dve(lambda e: e.tensor_copy(kg("Y_re"), PS[1][0:64, 1024:1088]))
        dve(lambda e: e.tensor_copy(kg("Y_im"), PS[1][0:64, 1536:1600]))
        dve(lambda e: e.tensor_mul(kg("Z_re"), kg("W_re"), kg("Y_re")))
        dve(lambda e: e.tensor_mul(E64, kg("W_im"), kg("Y_im")))
        dve(lambda e: e.tensor_sub(kg("Z_re"), kg("Z_re"), E64))
        dve(lambda e: e.tensor_mul(kg("Z_im"), kg("W_re"), kg("Y_im")))
        dve(lambda e: e.tensor_mul(E64, kg("W_im"), kg("Y_re")))
        z_done = dve(lambda e: e.tensor_add(kg("Z_im"), kg("Z_im"), E64), inc=True)
        P.wait("tensor", s_dve, z_done)
        pe(lambda e: e.matmul(PS[1][0:64, 1024:1088], kg("E2_re"), kg("Z_re"), start=True, stop=False))
        pe_n = pe(lambda e: e.matmul(PS[1][0:64, 1024:1088], kg("E2_imn"), kg("Z_im"), start=False, stop=True), inc=True)
        P.wait("scalar", s_pe, pe_n)
        kmat_done = act(lambda e: e.copy(kmat_bf[:], PS[1][0:64, 1024:1088]), inc=True)
        P.wait("sync", s_dve, zrow_done)
        dma("sync", lambda e: e.dma_start(out=bass.AP(kpad, 0, [[127, 1], [1, 127]]),
                                          in_=zrow[0:1, 0:127]), s_kp)
        P.wait("sync", s_act, kmat_done)
        dma("sync", lambda e: e.dma_start(out=bass.AP(kpad, 127, [[64, 64], [1, 64]]),
                                          in_=kmat_bf[:]), s_kp)
        P.wait("sync", s_kp, s_kp.n)
        hank_done = dma("sync", lambda e: e.dma_start(
            out=hank.ap()[:, 0:D].rearrange("q (d f) -> q d f", d=CH),
            in_=bass.AP(kpad, 0, [[1, 128], [128, CH], [1, 128]])), s_kp)
        P.wait("tensor", s_kp, hank_done)
        tt_ev = [0, 0]
        for i in range(8):
            sl = slice(i * 512, (i + 1) * 512)
            if tt_ev[i % 2]:
                P.wait("tensor", s_act, tt_ev[i % 2])
            pe_n = pe(lambda e, sl=sl, i=i: e.matmul(psv(1, i % 2), prev_sb[:], hank[:, sl],
                                                start=True, stop=True), inc=True)
            P.wait("scalar", s_pe, pe_n)
            tt_ev[i % 2] = act(lambda e, sl=sl, i=i: e.copy(ttoep[:, sl], psv(1, i % 2)), inc=True)
        ttoep_done = s_act.n
        bb_ev = [0, 0]
        for i in range(8):
            sl = slice(i * 512, (i + 1) * 512)
            if bb_ev[i % 2]:
                P.wait("tensor", s_dve, bb_ev[i % 2])
            pe_n = pe(lambda e, sl=sl, i=i: e.matmul(psv(0, i % 2), ones_sb[:],
                                                     bias_b[0:1, sl], start=True, stop=True), inc=True)
            P.wait("vector", s_pe, pe_n)
            bb_ev[i % 2] = dve(lambda e, sl=sl, i=i: e.tensor_copy(bias_b[:, sl], psv(0, i % 2)), inc=True)
        for bb in range(2):
            for g in range(4):
                set_bank(bb, g, "dve", s_dve.n)
        set_bank(1, 0, "act", ttoep_done)
        set_bank(1, 1, "act", ttoep_done)
        set_bank(1, 2, "act", kmat_done)
        set_bank(1, 3, "dve", s_dve.n)

        prepass_state = {}
        s_fst = [mksem("s_fst0"), mksem("s_fst1")]
        s_ppf = [mksem("s_ppf0"), mksem("s_ppf1")]

        def emit_prepass(upto=2 * CH):
            st = prepass_state
            if "sign_done" not in st:
                st["sign_done"] = {}
                st["ffn_win"] = [None, None]
                st["next_k"] = 0
                P.wait("gpsimd", s_ld, LD_FINAL)
            k0 = st["next_k"]
            k1 = min(upto, 2 * CH)
            st["next_k"] = max(k0, k1)
            stage_f32 = [qkv_sb[0][:].bitcast(F32)[:, 0:2048],
                         osc_bf[1][:].bitcast(F32)[:, 0:2048]]
            sign_done = st["sign_done"]
            for k in range(k0, max(k0, k1)):
                w, c = divmod(k, CH)
                if k >= 2:
                    enga, sga = sign_done[k - 2]
                    P.wait("gpsimd", s_act if enga == "act" else s_dve, sga)
                dma("gpsimd", lambda e, c=c, w=w, k=k: e.dma_start(
                    out=stage_f32[k % 2],
                    in_=wffnT.ap()[c * 128:(c + 1) * 128, w * 2048:(w + 1) * 2048]), s_ppf[k % 2])
                if k % 2 == 0:
                    P.wait("scalar", s_ppf[0], s_ppf[0].n)
                    if k >= 2:
                        P.wait("scalar", s_fst[0], (k // 2) * 16)
                    sg = act(lambda e: e.activation(
                        osc_bf[0][:, 0:2048], stage_f32[0], AF.Sign), inc=True)
                    sign_done[k] = ("act", sg)
                else:
                    P.wait("vector", s_ppf[1], s_ppf[1].n)
                    if k >= 2:
                        P.wait("vector", s_fst[1], (k // 2) * 16)
                    dve(lambda e: e.tensor_scalar(osc_bf[0][:, 2048:4096],
                                                  stage_f32[1], 0.0, None, ALU.is_gt))
                    sg = dve(lambda e: e.tensor_scalar(
                        osc_bf[0][:, 2048:4096], osc_bf[0][:, 2048:4096],
                        2.0, -1.0, ALU.mult, ALU.add), inc=True)
                    sign_done[k] = ("dve", sg)
                eng, sg = sign_done[k]
                P.wait("scalar", s_act if eng == "act" else s_dve, sg)
                dma("scalar", lambda e, c=c, w=w, k=k: e.dma_start(
                    out=ffn_bf.ap()[c * 128:(c + 1) * 128, w * 2048:(w + 1) * 2048],
                    in_=osc_bf[0][:, (k % 2) * 2048:(k % 2) * 2048 + 2048]), s_fst[k % 2])
                if c == CH - 1:
                    st["ffn_win"][w] = (s_fst[0].n, s_fst[1].n)
            if k1 < 2 * CH or "strips" in st:
                return
            st["strips"] = True
            for w in range(3):
                dma("gpsimd", lambda e, w=w: e.dma_start(
                    out=bass.AP(qkv_bf, w * 2048, [[QKVW, D], [1, 2048]]),
                    in_=bass.AP(wqkvT, w * 2048, [[QKVW, D], [1, 2048]])), s_ppq[w])
            for w in range(2):
                dma("gpsimd", lambda e, w=w: e.dma_start(
                    out=bass.AP(proj_bf, w * 2048, [[D, D], [1, 2048]]),
                    in_=bass.AP(wprojT, w * 2048, [[D, D], [1, 2048]])), s_ppp[w])

        wstate = {"n": 0, "loads": [0] * NWBUF}
        stream_gate = set()

        WQUEUE = ["sync", "sync", "sync"]

        def stream_sub(src_dram, row0, col0, ncols, gate_key):
            n = wstate["n"]
            bi = n % NWBUF
            qn = WQUEUE[bi]
            gk = (gate_key, col0 // COLW, qn)
            w = col0 // COLW
            if gate_key == "ffn":
                kneed = w * CH + row0 // 128 + CPS - 1
                ke = kneed if kneed % 2 == 0 else kneed - 1
                ko = kneed if kneed % 2 == 1 else kneed - 1
                if ke >= 0:
                    P.wait(qn, s_fst[0], (ke // 2 + 1) * 16)
                if ko >= 0:
                    P.wait(qn, s_fst[1], (ko // 2 + 1) * 16)
            elif gk not in stream_gate:
                if gate_key == "qkv":
                    P.wait(qn, s_ppq[w], 16)
                else:
                    P.wait(qn, s_ppp[w], 16)
                stream_gate.add(gk)
            if n >= NWBUF:
                P.wait(qn, s_wfree, n - (NWBUF - 1))
            wstate["loads"][bi] += 1
            wait_val = wstate["loads"][bi] * 16
            Wfull = src_dram.shape[1]
            srcap = bass.AP(src_dram, row0 * Wfull + col0,
                            [[Wfull, 128], [128 * Wfull, CPS], [1, ncols]])
            dstap = wbuf[bi].ap()[:, 0:CPS * ncols].rearrange("p (u n) -> p u n", u=CPS)
            dma(qn, lambda e, s=srcap, d=dstap: e.dma_start(out=d, in_=s), s_w[bi])
            wstate["n"] += 1
            return bi, wait_val

        def emit_pass(src_dram, win, blocks_actT, gate_key):
            col0 = win * COLW
            for q in range(CH // CPS):
                bi, wv = stream_sub(src_dram, q * CPS * 128, col0, COLW, gate_key)
                P.wait("tensor", s_w[bi], wv)
                for cc in range(CPS):
                    c = q * CPS + cc
                    for bb in range(2):
                        for g in range(4):
                            if c == 0:
                                gate_bank(bb, g)
                            last = (cc == CPS - 1 and bb == 1 and g == 3)
                            fn = (lambda e, c=c, bb=bb, g=g, bi=bi, cc=cc:
                                  e.matmul(psv(bb, g),
                                           blocks_actT[bb][:, c * 128:(c + 1) * 128],
                                           wbuf[bi][:, cc * COLW + g * 512:cc * COLW + (g + 1) * 512],
                                           start=(c == 0), stop=(c == CH - 1)))
                            if last:
                                s_wfree.n += 1
                                P.emit("tensor", lambda e, f=fn: f(e).then_inc(s_wfree.h, 1))
                            else:
                                P.emit("tensor", fn)
            return s_wfree.n

        halves_ready = {}

        def stats_half(i, h, epi_gate):
            o = i * 12
            cols = slice(h * 2048, (h + 1) * 2048)
            P.wait("scalar", s_dve, epi_gate)
            act(lambda e, i=i, cols=cols, o=o, h=h: e.activation(
                ln_bf[:, 0:2048], res[i][:, cols], AF.Copy,
                accum_out=stat[:, o + 2 * h:o + 2 * h + 1]))
            v = act(lambda e, i=i, cols=cols, o=o, h=h: e.activation(
                ln_bf[:, 0:2048], res[i][:, cols], AF.Square,
                accum_out=stat[:, o + 2 * h + 1:o + 2 * h + 2]), inc=True)
            halves_ready.setdefault(i, {})[h] = v

        def ln_stats(i, with_rd):
            o = i * 12
            x = res[i]
            hv = halves_ready.pop(i, None)
            P.wait("scalar", s_dve, s_dve.n)
            P.wait("scalar", s_pe, lnbf_pe[0])
            P.wait("vector", s_pe, lnbf_pe[0])
            if hv is None:
                hvv = act(lambda e, o=o: e.activation(
                    ln_bf[:], x[:], AF.Square,
                    accum_out=stat[:, o + 1:o + 2]), inc=True)
                dve(lambda e, o=o: e.reduce_sum(stat[:, o:o + 1], x[:], axis=AX.X))
            else:
                hvv = max(hv.values())
                P.wait("vector", s_act, hvv)
                dve(lambda e: e.tensor_add(stat[:, o:o + 1], stat[:, o:o + 1], stat[:, o + 2:o + 3]))
                dve(lambda e: e.tensor_add(stat[:, o + 1:o + 2], stat[:, o + 1:o + 2], stat[:, o + 3:o + 4]))
            P.wait("vector", s_act, hvv)
            dve(lambda e: e.tensor_scalar(stat[:, o + 4:o + 5], stat[:, o:o + 1], 1.0 / D, None, ALU.mult))
            dve(lambda e: e.tensor_mul(stat[:, o + 9:o + 10], stat[:, o + 4:o + 5], stat[:, o + 4:o + 5]))
            v_ok = dve(lambda e: e.scalar_tensor_tensor(stat[:, o + 8:o + 9], stat[:, o + 1:o + 2],
                                                        1.0 / D, stat[:, o + 9:o + 10],
                                                        ALU.mult, ALU.subtract), inc=True)
            P.wait("scalar", s_dve, v_ok)
            act(lambda e: e.activation(stat[:, o + 9:o + 10], stat[:, o + 8:o + 9],
                                       AF.Ln, bias=epscol[:]))
            r_ok = act(lambda e: e.activation(stat[:, o + 5:o + 6], stat[:, o + 9:o + 10],
                                              AF.Exp, scale=-0.5), inc=True)
            P.wait("vector", s_act, r_ok)
            if with_rd:
                dve(lambda e: e.tensor_mul(stat[:, o + 9:o + 10], stat[:, o + 5:o + 6], dcol[:]))
                dve(lambda e: e.tensor_scalar(stat[:, o + 7:o + 8], stat[:, o + 9:o + 10],
                                              1.0, None, ALU.add))
                dve(lambda e: e.tensor_mul(stat[:, o + 6:o + 7], stat[:, o + 4:o + 5],
                                           stat[:, o + 9:o + 10]))
            outs = []
            for h in (0, 1):
                cols = slice(h * 2048, (h + 1) * 2048)
                outs.append(dve(lambda e, cols=cols, o=o: e.tensor_scalar(
                    ln_bf[:, cols], x[:, cols], stat[:, o + 4:o + 5],
                    stat[:, o + 5:o + 6], ALU.subtract, ALU.mult), inc=True))
            return tuple(outs)

        def transposes(i, src_fn, reads_lnbf=True, half_gates=None):
            evs = {}
            for c in range(CH):
                view = ps_bf(0, c % 4)
                if half_gates is not None and c % 16 == 0:
                    P.wait("tensor", s_dve, half_gates[c // 16])
                if c >= 4:
                    which, val = evs[c - 4]
                    P.wait("tensor", s_dve if which == 0 else s_act, val)
                pe_n = pe(lambda e, v=view, sap=src_fn(c): e.transpose(v, sap, ident_sb[:]),
                          inc=True)
                if c % 2 == 0:
                    P.wait("vector", s_pe, pe_n)
                    val = dve(lambda e, c=c, v=view: e.tensor_copy(
                        actT[i][:, c * 128:(c + 1) * 128], v), inc=True)
                    evs[c] = (0, val)
                else:
                    P.wait("scalar", s_pe, pe_n)
                    val = act(lambda e, c=c, v=view: e.copy(
                        actT[i][:, c * 128:(c + 1) * 128], v), inc=True)
                    evs[c] = (1, val)
            if reads_lnbf:
                lnbf_pe[0] = s_pe.n
            for g in range(4):
                wg, vg = evs[CH - 4 + g]
                set_bank(0, g, "dve" if wg == 0 else "act", vg)
            return s_dve.n, s_act.n

        def sub_conv(T, prestats=False):
            for i in (0, 1):
                hg = ln_stats(i, with_rd=True)
                gate_banks_for_pe()
                transposes(i, lambda c: ln_bf[:, c * 128:(c + 1) * 128], half_gates=hg)
            tp_d, tp_a = s_dve.n, s_act.n
            for qd in range(2):
                P.wait("tensor", s_dve, tp_d)
                P.wait("tensor", s_act, tp_a)
                gate_banks_for_pe()
                maxJ = 4 * (4 * qd + 3) + 3
                work = []
                for J in range(0, min(CH, maxJ + 1)):
                    for bb in range(2):
                        for g in range(4):
                            cgg = 4 * qd + g
                            if J > 4 * cgg + 3:
                                continue
                            if J <= 4 * cgg:
                                w, roff, poff = 512, 128 * (4 * cgg - J), 0
                            else:
                                w = 512 - 128 * (J - 4 * cgg)
                                roff, poff = 0, 128 * (J - 4 * cgg)
                            work.append((bb, g, J, w, roff, poff, 4 * cgg + 3))
                for idx, (bb, g, J, w, roff, poff, lastJ) in enumerate(work):
                    fn = (lambda e, bb=bb, g=g, J=J, w=w, roff=roff, poff=poff, lastJ=lastJ:
                          e.matmul(psv(bb, g, off=poff, w=w),
                                   actT[bb][:, J * 128:(J + 1) * 128],
                                   ttoep[:, roff:roff + w],
                                   start=(J == 0), stop=(J == lastJ)))
                    conv_pe = pe(fn, inc=(idx == len(work) - 1))
                P.wait("vector", s_pe, conv_pe)
                for bb in range(2):
                    o = bb * 12
                    cols = slice(qd * 2048, (qd + 1) * 2048)
                    dve(lambda e, bb=bb, cols=cols, o=o: e.scalar_tensor_tensor(
                        tr2[:], res[bb][:, cols], stat[:, o + 7:o + 8], PS[bb][:, 0:2048],
                        ALU.mult, ALU.add))
                    v = dve(lambda e, bb=bb, cols=cols, o=o: e.tensor_scalar(
                        res[bb][:, cols], tr2[:], stat[:, o + 6:o + 7], None,
                        ALU.subtract), inc=True)
                    for g in range(4):
                        set_bank(bb, g, "dve", v)
                    if prestats:
                        stats_half(bb, qd, s_dve.n)

        def stage1_ffn(i):
            hg = ln_stats(i, with_rd=False)
            gate_banks_for_pe()
            tp_d, tp_a = transposes(i, lambda c: ln_bf[:, c * 128:(c + 1) * 128],
                                    half_gates=hg)
            P.wait("tensor", s_dve, tp_d)
            P.wait("tensor", s_act, tp_a)
            gate_bank(1, 0)
            for c in range(CH):
                lg_pe = pe(lambda e, c=c: e.matmul(PS[1][:, 0:8],
                                           actT[i][:, c * 128:(c + 1) * 128],
                                           gate_sb[:, c * NEXP:(c + 1) * NEXP],
                                           start=(c == 0), stop=(c == CH - 1)),
                           inc=(c == CH - 1))
            P.wait("vector", s_pe, lg_pe)
            dve(lambda e: e.reduce_max(alph[:, 0:1], PS[1][:, 0:8], axis=AX.X))
            nm_ok = dve(lambda e: e.tensor_scalar(alph[:, 1:2], alph[:, 0:1], -1.0,
                                                  None, ALU.mult), inc=True)
            P.wait("scalar", s_dve, nm_ok)
            ex_ok = act(lambda e: e.activation(rout_f[:], PS[1][:, 0:8], AF.Exp,
                                               bias=alph[:, 1:2]), inc=True)
            set_bank(1, 0, "act", ex_ok)
            P.wait("vector", s_act, ex_ok)
            dve(lambda e: e.reduce_sum(alph[:, 0:1], rout_f[:], axis=AX.X))
            dve(lambda e: e.reciprocal(alph[:, 0:1], alph[:, 0:1]))
            gate_bank(1, 1)
            P.wait("vector", s_pe, last_rtp[0])
            rt_ok = dve(lambda e: e.tensor_scalar(rout_bf[:], rout_f[:], alph[:, 0:1],
                                                  None, ALU.mult), inc=True)
            P.wait("tensor", s_dve, rt_ok)
            last_rtp[0] = last_rtp[0]
            rtp = pe(lambda e: e.transpose(PS[1][:].bitcast(BF16)[0:8, 2048:2176],
                                           rout_bf[:], ident_sb[:]), inc=True)
            last_rtp[0] = rtp
            P.wait("scalar", s_pe, rtp)
            rT_ok = act(lambda e, i=i: e.copy(routT[i][:], PS[1][:].bitcast(BF16)[0:8, 2048:2176]), inc=True)
            set_bank(1, 1, "act", rT_ok)
            P.wait("tensor", s_act, rT_ok)
            INSLOT = [(0, 2), (1, 2), (0, 3), (1, 3)]
            ins_mults = [0, 0, 0, 0]
            for sl in INSLOT:
                gate_bank(*sl)
            for grp in range(8):
                pb, gsl = INSLOT[grp % 4]
                if ins_mults[grp % 4]:
                    P.wait("tensor", s_dve, ins_mults[grp % 4])
                for cc in range(4):
                    c = grp * 4 + cc
                    g_pe = pe(lambda e, c=c, cc=cc, i=i, pb=pb, gsl=gsl: e.matmul(
                        psv(pb, gsl, off=cc * 128, w=128),
                        ics_sb[:, c * 128:(c + 1) * 128], routT[i][:],
                        start=True, stop=True), inc=(cc == 3))
                P.wait("vector", s_pe, g_pe)
                ins_mults[grp % 4] = dve(lambda e, grp=grp, i=i, pb=pb, gsl=gsl: e.tensor_mul(
                    actT[i][:, grp * 512:(grp + 1) * 512],
                    actT[i][:, grp * 512:(grp + 1) * 512],
                    psv(pb, gsl)), inc=True)
                set_bank(pb, gsl, "dve", ins_mults[grp % 4])
            zT_d = s_dve.n
            osc_ev = [0, 0]
            gate_bank(0, 3)
            gate_bank(1, 3)
            if not osc_gate[0]:
                P.wait("scalar", s_fst[0], prepass_state["ffn_win"][1][0])
                P.wait("scalar", s_fst[1], prepass_state["ffn_win"][1][1])
                osc_gate[0] = True
            for gg in range(8):
                pb = gg % 2
                if osc_ev[pb]:
                    P.wait("tensor", s_act, osc_ev[pb])
                po = pe(lambda e, gg=gg, i=i, pb=pb: e.matmul(
                    psv(pb, 3), routT[i][:], ocs_sb[:, gg * 512:(gg + 1) * 512],
                    start=True, stop=True), inc=True)
                P.wait("scalar", s_pe, po)
                osc_ev[pb] = act(lambda e, gg=gg, i=i, pb=pb: e.copy(
                    osc_bf[i][:, gg * 512:(gg + 1) * 512], psv(pb, 3)), inc=True)
                set_bank(pb, 3, "act", osc_ev[pb])
            return zT_d, s_act.n

        def sub_ffn(T, prestats=False):
            epi_b = [0, 0]
            for i in (0, 1):
                stage1_ffn(i)
            for bb in range(2):
                dve(lambda e, bb=bb: e.tensor_add(res[bb][:], res[bb][:], bias_b[:]))
            for win in range(2):
                wf = emit_pass(ffn_bf, win, actT, "ffn")
                P.wait("vector", s_wfree, wf)
                for bb in range(2):
                    cols = slice(win * 2048, (win + 1) * 2048)
                    dve(lambda e, bb=bb, cols=cols: e.tensor_mul(
                        tr2[:], PS[bb][:, 0:2048], osc_bf[bb][:, cols]))
                    v = dve(lambda e, bb=bb, cols=cols: e.tensor_add(
                        res[bb][:, cols], res[bb][:, cols], tr2[:]), inc=True)
                    for g in range(4):
                        set_bank(bb, g, "dve", v)
                    if prestats:
                        stats_half(bb, win, s_dve.n)
                    epi_b[bb] = s_dve.n
            return epi_b

        def sub_attn(T, prestats=False):
            for i in (0, 1):
                hg = ln_stats(i, with_rd=False)
                gate_banks_for_pe()
                transposes(i, lambda c: ln_bf[:, c * 128:(c + 1) * 128], half_gates=hg)
            tp_d, tp_a = s_dve.n, s_act.n
            P.wait("tensor", s_dve, tp_d)
            P.wait("tensor", s_act, tp_a)
            kviewS = lambda b, k0, nk: bass.AP(qkv_sb[b].ap().tensor, k0 * 768 + 512,
                                               [[QKVW, 128], [768, nk], [1, 128]])
            vviewS = lambda b, k0, nk: bass.AP(qkv_sb[b].ap().tensor, k0 * 768 + 640,
                                               [[QKVW, 128], [768, nk], [1, 128]])
            qviewS = lambda b, g, k0, nk: bass.AP(qkv_sb[b].ap().tensor, k0 * 768 + g * 128,
                                                  [[QKVW, 128], [768, nk], [1, 128]])
            s_at = s_atn

            def poolop(fn):
                P.wait("gpsimd", s_at, s_at.n)
                s_at.n += 1
                P.emit("gpsimd", lambda e, fn=fn: fn(e).then_inc(s_at.h, 1))
                return s_at.n

            SUBSETS = {0: (0, 2), 1: (2, 3), 2: (5, 3)}
            out0_ok = 0
            for win in range(3):
                wf = emit_pass(qkv_bf, win, actT, "qkv")
                P.wait("scalar", s_wfree, wf)
                for bb in range(2):
                    cols = slice(win * 2048, (win + 1) * 2048)
                    v = act(lambda e, bb=bb, cols=cols: e.copy(
                        qkv_sb[bb][:, cols], PS[bb][:, 0:2048]), inc=True)
                    for g in range(4):
                        set_bank(bb, g, "act", v)
                qkv_ev = s_act.n
                k0, nk = SUBSETS[win]
                koff = k0 * 128
                P.wait("gpsimd", s_act, qkv_ev)
                poolop(lambda e, k0=k0, nk=nk, koff=koff: e.tensor_sub(
                    dkbuf[:, koff:koff + nk * 128], kviewS(1, k0, nk), kviewS(0, k0, nk)))
                poolop(lambda e, k0=k0, nk=nk, koff=koff: e.tensor_sub(
                    dvbuf[:, koff:koff + nk * 128], vviewS(1, k0, nk), vviewS(0, k0, nk)))
                for g in range(4):
                    pk = poolop(lambda e, g=g, k0=k0, nk=nk, koff=koff: e.tensor_mul(
                        prodb[:, 0:nk * 128], qviewS(1, g, k0, nk),
                        dkbuf[:, koff:koff + nk * 128]))
                    P.wait("vector", s_at, pk)
                    dve(lambda e, g=g, k0=k0, nk=nk: e.tensor_reduce(
                        bass.AP(alph.ap().tensor, 4 * k0 + g, [[64, 128], [4, nk]]),
                        prodb.ap()[:, 0:nk * 128].rearrange("p (k d) -> p k d", k=nk),
                        axis=AX.X, op=ALU.add), inc=True)
                    if g < 3:
                        P.wait("gpsimd", s_dve, s_dve.n)
                sc_ok = s_dve.n
                P.wait("scalar", s_dve, sc_ok)
                acols = slice(32 + 4 * k0, 32 + 4 * (k0 + nk))
                scols = slice(4 * k0, 4 * (k0 + nk))
                al_ok = act(lambda e, acols=acols, scols=scols: e.activation(
                    alph[:, acols], alph[:, scols], AF.Exp, scale=-ATT_SCALE), inc=True)
                P.wait("vector", s_act, al_ok)
                dve(lambda e, acols=acols: e.tensor_scalar(alph[:, acols], alph[:, acols],
                                                           1.0, None, ALU.add))
                sig_ok = dve(lambda e, acols=acols: e.reciprocal(alph[:, acols], alph[:, acols]),
                             inc=True)
                for g in range(4):
                    out0_ok = act(lambda e, g=g, k0=k0, nk=nk: e.copy(
                        qviewS(0, g, k0, nk), vviewS(0, k0, nk)), inc=True)
                P.wait("gpsimd", s_dve, sig_ok)
                for g in range(4):
                    avw = bass.AP(alph.ap().tensor, 32 + 4 * k0 + g, [[64, 128], [4, nk], [0, 128]])
                    poolop(lambda e, g=g, avw=avw, k0=k0, nk=nk, koff=koff: e.tensor_mul(
                        prodb[:, 0:nk * 128], dvbuf[:, koff:koff + nk * 128], avw))
                    poolop(lambda e, g=g, k0=k0, nk=nk: e.tensor_add(
                        qviewS(1, g, k0, nk), prodb[:, 0:nk * 128], vviewS(0, k0, nk)))
            at_ok = s_at.n
            P.wait("tensor", s_at, at_ok)
            P.wait("tensor", s_act, out0_ok)
            for i in (0, 1):
                P.wait("tensor", s_atn, at_ok)
                gate_banks_for_pe()
                transposes(i, lambda c, i=i: qkv_sb[i][:, (c // 4) * 768 + (c % 4) * 128:(c // 4) * 768 + (c % 4) * 128 + 128], reads_lnbf=False)
            tp2_d, tp2_a = s_dve.n, s_act.n
            P.wait("tensor", s_dve, tp2_d)
            P.wait("tensor", s_act, tp2_a)
            for win in range(2):
                wf = emit_pass(proj_bf, win, actT, "proj")
                P.wait("vector", s_wfree, wf)
                for bb in range(2):
                    cols = slice(win * 2048, (win + 1) * 2048)
                    v = dve(lambda e, bb=bb, cols=cols: e.tensor_add(
                        res[bb][:, cols], res[bb][:, cols], PS[bb][:, 0:2048]), inc=True)
                    for g in range(4):
                        set_bank(bb, g, "dve", v)
                    if prestats:
                        stats_half(bb, win, s_dve.n)

        for T in range(NT):
            if T > 0:
                for i in (0, 1):
                    s, r0 = (0, T * 128) if i == 0 else (1, T * 128)
                    P.wait("gpsimd", s_st[i], 16 * T)
                    dma("gpsimd", lambda e, i=i, s=s, r0=r0: e.dma_start(
                        out=res[i][:], in_=x_in.ap()[s, r0:r0 + 128, :]), s_x)
            x_ready = s_x.n
            for eng in ("vector", "scalar"):
                P.wait(eng, s_x, x_ready)
            if SUBLAYERS[0]:
                sub_conv(T, prestats=SUBLAYERS[1])
            if T == 0:
                emit_prepass()
            if SUBLAYERS[1]:
                sub_ffn(T, prestats=SUBLAYERS[2])
            if SUBLAYERS[2]:
                sub_attn(T, prestats=SUBLAYERS[3])
            if SUBLAYERS[3]:
                epi_b = sub_ffn(T)
            else:
                epi_b = [s_dve.n, s_dve.n]
            for i in (0, 1):
                s, r0 = (0, T * 128) if i == 0 else (1, T * 128)
                P.wait("gpsimd", s_dve, epi_b[i])
                dma("gpsimd", lambda e, i=i, s=s, r0=r0: e.dma_start(
                    out=x_out.ap()[s, r0:r0 + 128, :], in_=res[i][:]), s_st[i])

        P.wait("sync", s_st[0], s_st[0].n)
        P.wait("sync", s_st[1], s_st[1].n)

        with nc.Block() as block:
            P.replay(block)

    return nc


def kernel(**inputs):
    consts = _host_consts()
    u = np.ascontiguousarray(inputs["L0_attn_a_u"])
    lam = inputs["L0_attn_a_Lambda_ri"]
    p_ri = inputs["L0_attn_a_p_ri"]
    q_ri = inputs["L0_attn_a_q_ri"]
    vc = inputs["L0_attn_a_Vc_ri"]
    ct = inputs["L0_attn_a_Ct_ri"]
    f32c = lambda a: np.ascontiguousarray(np.asarray(a), dtype=np.float32)
    base = {
        "wqkvT": f32c(np.asarray(inputs["L0_attn_b_weight_qkv"]).T),
        "wprojT": f32c(np.asarray(inputs["L0_attn_b_weight_proj"]).T),
        "wffnT": f32c(np.asarray(inputs["L0_ffn_weight"]).T),
        "gateT": f32c(np.asarray(inputs["L0_ffn_gate_w"]).T),
        "ics": f32c(inputs["L0_ffn_in_channel_scale"]),
        "ocs": f32c(inputs["L0_ffn_out_channel_scale"]),
        "bias_in": f32c(np.asarray(inputs["L0_ffn_bias"]).reshape(1, -1)),
        "lam_re": f32c(np.asarray(lam)[:, 0].reshape(64, 1)),
        "lam_im": f32c(np.asarray(lam)[:, 1].reshape(64, 1)),
        "p_re": f32c(np.asarray(p_ri)[:, 0].reshape(64, 1)),
        "p_im": f32c(np.asarray(p_ri)[:, 1].reshape(64, 1)),
        "q_re": f32c(np.asarray(q_ri)[:, 0].reshape(64, 1)),
        "q_im": f32c(np.asarray(q_ri)[:, 1].reshape(64, 1)),
        "ct_re": f32c(np.asarray(ct)[0, :, 0].reshape(64, 1)),
        "ct_im": f32c(np.asarray(ct)[0, :, 1].reshape(64, 1)),
        "vcT_re": f32c(np.asarray(vc)[:, :, 0].T),
        "vcT_im": f32c(np.asarray(vc)[:, :, 1].T),
        "bp_in": f32c(np.asarray(inputs["L0_attn_a_Bp"]).reshape(64, 1)),
        "d_in": f32c(np.asarray(inputs["L0_attn_a_D"]).reshape(1, 1)),
        "ls_in": f32c(np.asarray(inputs["L0_attn_a_log_step"]).reshape(1, 1)),
    }
    base.update(consts)

    nc = build_program()
    in_maps = []
    for c in range(NCORE):
        m = dict(base)
        m["x_in"] = np.ascontiguousarray(u[:, c * 1024:(c + 1) * 1024, :])
        in_maps.append(m)
    res = run_bass_kernel_spmd(nc, in_maps, core_ids=list(range(NCORE)))
    out = np.concatenate([r["x_out"] for r in res.results], axis=1)
    return np.ascontiguousarray(out, dtype=np.float32)
